# version 16
# baseline (speedup 1.0000x reference)
import numpy as np
from contextlib import ExitStack

import concourse.bass as bass
import concourse.mybir as mybir
from concourse.bass_utils import run_bass_kernel_spmd
from concourse.alu_op_type import AluOpType as ALU

F32 = mybir.dt.float32
BF16 = mybir.dt.bfloat16
AF = mybir.ActivationFunctionType
AX = mybir.AxisListType.X

D = 2048
S = 2048
NSEQ = 2
DCH = 16
TBN = 4
TW = 512
HEADS = 16
HGN = 4
HGS = 4
DKV = 512
CCH = 4
FF = 4096
FCH = 32
SHIFT = 30.0
NLN128 = -float(np.log(128.0) * 0.5)

ENGINES = ("sync", "tensor", "vector", "scalar", "gpsimd")


class Ctr:
    def __init__(self, handle, name):
        self.h = handle
        self.name = name
        self.n = 0


class Prog:
    def __init__(self):
        self.ops = {e: [] for e in ENGINES}
        self._lastwait = {e: {} for e in ENGINES}

    def emit(self, eng, fn, waits=(), inc=None, inc_n=1):
        lw = self._lastwait[eng]
        wl = []
        for w in waits:
            if w is None:
                continue
            c, v = w
            if v > 0 and v > lw.get(c.name, 0):
                wl.append((c, v))
                lw[c.name] = v
        self.ops[eng].append((tuple(wl), fn, inc, inc_n))
        if inc is not None:
            inc.n += inc_n
        return inc.n if inc is not None else None

    def pe(self, fn, waits=(), inc=None):
        return self.emit("tensor", fn, waits, inc, 1)

    def dve(self, fn, waits=(), inc=None):
        return self.emit("vector", fn, waits, inc, 1)

    def act(self, fn, waits=(), inc=None):
        return self.emit("scalar", fn, waits, inc, 1)

    def dma_s(self, fn, waits=(), ctr=None):
        return self.emit("sync", fn, waits, ctr, 16)

    def dma_g(self, fn, waits=(), ctr=None):
        return self.emit("gpsimd", fn, waits, ctr, 16)

    def join(self, ctrs):
        snap = [(c, c.n) for c in ctrs]
        for e in ENGINES:
            for (c, v) in snap:
                if v > 0 and v > self._lastwait[e].get(c.name, 0):
                    self.ops[e].append((((c, v),), None, None, 0))
                    self._lastwait[e][c.name] = v

    def replay(self, block):
        def mk(eng_name):
            def body(eng):
                for wl, fn, inc, n in self.ops[eng_name]:
                    for (c, v) in wl:
                        eng.wait_ge(c.h, v)
                    if fn is None:
                        continue
                    inst = fn(eng)
                    if inc is not None:
                        inst.then_inc(inc.h, n)
            return body
        block.sync(mk("sync"))
        block.tensor(mk("tensor"))
        block.vector(mk("vector"))
        block.scalar(mk("scalar"))
        block.gpsimd(mk("gpsimd"))


class Res:

    def __init__(self, t=None):
        self.t = t
        self.ready = None
        self.rguards = []

    def w_waits(self):
        w = list(self.rguards)
        if self.ready:
            w.append(self.ready)
        self.rguards = []
        return w

    def set_ready(self, ctr):
        self.ready = (ctr, ctr.n)

    def r_wait(self):
        return [self.ready] if self.ready else []

    def add_read(self, ctr):
        self.rguards.append((ctr, ctr.n))


def build_program(n_seq=NSEQ, n_hg=HGN, n_qb=TBN, do_attn=True, do_ffn=True, dbg=()):
    nc = bass.Bass("TRN2", target_bir_lowering=False)
    dbg = set(dbg)

    AP = lambda name, shape, dt, kind: nc.dram_tensor(name, shape, dt, kind=kind).ap()
    I, O, N_ = "ExternalInput", "ExternalOutput", "Internal"

    hT4 = AP("hT4", [NSEQ, TBN, DCH, 128, TW], F32, I)
    wattn_img = AP("wattn_img", [HGN, 128, 4 * 1280], mybir.dt.bfloat16, N_)
    W_dkv = AP("W_dkv", [D, DKV], F32, I)
    W_dq = AP("W_dq", [D, DKV], F32, I)
    W_uk = AP("W_uk", [DKV, 1024], F32, I)
    W_uv = AP("W_uv", [DKV, 2048], F32, I)
    W_uq = AP("W_uq", [DKV, 1024], F32, I)
    W_qr = AP("W_qr", [DKV, 1024], F32, I)
    W_kr = AP("W_kr", [D, 1024], F32, I)
    W0a4 = AP("W0a4", [FCH, 128, DCH, 128], F32, I)
    W1a4 = AP("W1a4", [FCH, 128, DCH, 128], F32, I)
    W0b4 = AP("W0b4", [DCH, 128, FCH, 128], F32, I)
    W1b4 = AP("W1b4", [DCH, 128, FCH, 128], F32, I)
    Wo4 = AP("Wo4", [DCH, 128, DCH, 128], F32, I)
    bcin = {nm: AP("bc_" + nm, [128, f // 128], F32, I) for nm, f in (
        ("dkv", DKV), ("dq", DKV), ("uk", 1024), ("uq", 1024), ("qr", 1024),
        ("kr", 1024), ("a0", FF), ("a1", FF))}
    b_uv_r = AP("b_uv_r", [1, 2048], F32, I)
    b_o_r = AP("b_o_r", [1, D], F32, I)
    b0b_r = AP("b0b_r", [1, D], F32, I)
    b1b_r = AP("b1b_r", [1, D], F32, I)
    Wg_in = AP("Wg", [2, D], F32, I)
    cosT = AP("cosT", [128, S], F32, I)
    sinT = AP("sinT", [128, S], F32, I)

    outT = AP("outT", [NSEQ, TBN, DCH, 128, TW], F32, O)

    xn_dram = AP("xn_dram", [NSEQ, TBN, DCH, 128, TW], BF16, N_)
    attn_dram = AP("attn_dram", [NSEQ, TBN, DCH, 128, TW], BF16, N_)
    kr_dram = AP("kr_dram", [8, 128, S], BF16, N_)
    wdiff_dram = AP("wdiff_dram", [D], F32, N_)
    W0a4b = AP("W0a4b", [FCH, 128, DCH, 128], BF16, N_)
    W1a4b = AP("W1a4b", [FCH, 128, DCH, 128], BF16, N_)
    W0b4b = AP("W0b4b", [DCH, 128, FCH, 128], BF16, N_)
    W1b4b = AP("W1b4b", [DCH, 128, FCH, 128], BF16, N_)
    Wo4b = AP("Wo4b", [DCH, 128, DCH, 128], BF16, N_)

    dbg_aps = {}
    if "ckv" in dbg:
        dbg_aps["ckv"] = AP("dbg_ckv", [CCH, 128, S], F32, O)
    if "cq" in dbg:
        dbg_aps["cq"] = AP("dbg_cq", [CCH, 128, S], F32, O)
    if "kr" in dbg:
        dbg_aps["kr"] = AP("dbg_kr", [8, 128, S], F32, O)
    if "xn" in dbg:
        dbg_aps["xn"] = AP("dbg_xn", [NSEQ, TBN, DCH, 128, TW], F32, O)
    if "attn" in dbg:
        dbg_aps["attn"] = AP("dbg_attn", [NSEQ, TBN, DCH, 128, TW], F32, O)
    if "qk" in dbg:
        dbg_aps["K"] = AP("dbg_K", [128, HGS * S], mybir.dt.bfloat16, O)
        dbg_aps["Q"] = AP("dbg_Q", [128, HGS * TW], mybir.dt.bfloat16, O)
        dbg_aps["P"] = AP("dbg_P", [128, DCH * TW], mybir.dt.bfloat16, O)
        dbg_aps["klog"] = AP("dbg_klog", [1, HEADS], F32, O)
        dbg_aps["negc"] = AP("dbg_negc", [128, HGS], F32, O)
        dbg_aps["den"] = AP("dbg_den", [1, 4096], F32, O)
    if "gate" in dbg:
        dbg_aps["gate"] = AP("dbg_gate", [NSEQ, 1, S], mybir.dt.bfloat16, O)
        dbg_aps["grow"] = AP("dbg_grow", [NSEQ, 1, 4096], F32, O)
    if "dump" in dbg:
        dbg_aps["xnd"] = AP("dbg_xnd", [NSEQ, TBN, DCH, 128, TW], mybir.dt.bfloat16, O)
        dbg_aps["attnd"] = AP("dbg_attnd", [NSEQ, TBN, DCH, 128, TW], mybir.dt.bfloat16, O)
    if "hm" in dbg:
        dbg_aps["h0d"] = AP("dbg_h0d", [NSEQ, FCH, 128, TW], mybir.dt.bfloat16, O)
        dbg_aps["h1d"] = AP("dbg_h1d", [NSEQ, FCH, 128, TW], mybir.dt.bfloat16, O)
    if "ostep" in dbg:
        dbg_aps["wov"] = AP("dbg_wov", [NSEQ, 128, 2048], mybir.dt.bfloat16, O)
        dbg_aps["w0v"] = AP("dbg_w0v", [NSEQ, 128, 2048], mybir.dt.bfloat16, O)
        dbg_aps["w1v"] = AP("dbg_w1v", [NSEQ, 128, 2048], mybir.dt.bfloat16, O)
    if "ffn" in dbg:
        dbg_aps["h0m"] = AP("dbg_h0m", [128, FCH * TW], mybir.dt.bfloat16, O)
        dbg_aps["h1m"] = AP("dbg_h1m", [128, FCH * TW], mybir.dt.bfloat16, O)
        dbg_aps["mbc"] = AP("dbg_mbc", [128, 2 * TW], mybir.dt.bfloat16, O)
        dbg_aps["xa"] = AP("dbg_xa", [128, DCH * TW], mybir.dt.bfloat16, O)

    prog = Prog()

    with ExitStack() as stack:
        en = stack.enter_context

        cS = Ctr(en(nc.semaphore("cS")), "cS")
        cG = Ctr(en(nc.semaphore("cG")), "cG")
        cPE = Ctr(en(nc.semaphore("cPE")), "cPE")
        cACT = Ctr(en(nc.semaphore("cACT")), "cACT")
        cDVE = Ctr(en(nc.semaphore("cDVE")), "cDVE")
        cSc = Ctr(en(nc.semaphore("cSc")), "cSc")
        cGc = Ctr(en(nc.semaphore("cGc")), "cGc")
        cR = Ctr(en(nc.semaphore("cR")), "cR")
        mkc = lambda nm: Ctr(en(nc.semaphore(nm)), nm)
        cH = [mkc("cH0"), mkc("cH1")]
        cW_ = mkc("cW")
        cWhg = mkc("cWhg")
        cKR = mkc("cKR")
        cXA = mkc("cXA")
        cWF = [mkc("cWF0"), mkc("cWF1")]
        cWO = [mkc("cWO0"), mkc("cWO1")]
        cW0B = [mkc("cW0B0"), mkc("cW0B1")]
        cW1B = [mkc("cW1B0"), mkc("cW1B1")]
        cHR = [mkc("cHR0"), mkc("cHR1")]
        cOS = [mkc("cOS0"), mkc("cOS1")]
        cAS = mkc("cAS")
        cXS = mkc("cXS")
        cKS = mkc("cKS")
        cBC = mkc("cBC")
        cIC = mkc("cIC")
        ALLC = None

        sb = lambda name, shape, dt: en(nc.sbuf_tensor(name, shape, dt))

        cos_b = sb("cos_b", [128, S], BF16)
        sin_b = sb("sin_b", [128, S], BF16)
        ones_cb = sb("ones_cb", [128, 1], BF16)
        ones_sq = sb("ones_sq", [128, 128], BF16)
        ones_rb = sb("ones_rb", [1, TW], BF16)
        ones_rf = sb("ones_rf", [1, 128], F32)
        wdiff_col = sb("wdiff_col", [128, DCH], F32)
        bc = {}
        for nm, f in (("dkv", DKV), ("dq", DKV), ("uk", 1024), ("uq", 1024),
                      ("qr", 1024), ("kr", 1024), ("a0", FF), ("a1", FF)):
            bc[nm] = sb("bcs_" + nm, [128, f // 128], F32)
        negc_sb = sb("negc_sb", [128, HGS], F32)
        kraw = sb("kraw", [128, TW], BF16)
        krot = sb("krot", [128, TW], BF16)
        t3_sb = sb("t3_sb", [128, TW], BF16)
        t4_sb = sb("t4_sb", [128, TW], BF16)
        rows = sb("rows", [1, 4096], F32)
        row_sum_bf = sb("row_sum_bf", [1, D], BF16)
        row_dif_bf = sb("row_dif_bf", [1, D], BF16)
        m_row_bf = sb("m_row_bf", [1, S], BF16)
        klog = sb("klog", [1, HEADS], F32)
        sumw = sb("sumw", [1, 1], F32)
        sc11 = sb("sc11", [1, 16], F32)

        ckv_sb = sb("ckv_sb", [128, CCH * S], BF16)
        cq_sb = sb("cq_sb", [128, CCH * S], BF16)

        ABF_N = 59392
        abf = sb("abf", [128, ABF_N], BF16)
        AF32_N = 2560
        af32 = sb("af32", [128, AF32_N], F32)

        bfv = lambda o, n: abf[:, o:o + n]
        f32v = lambda o, n: af32[:, o:o + n]

        A_xn = 0
        A_w = 8192
        A_h2 = 40960
        A_hbf = 41984

        T_K = 0
        T_V = 8192
        T_Q = 16384
        T_P0 = 18432
        T_P1 = 26624
        T_atn = 34816
        T_whg = 36864
        T_tree = 41984
        T_ksq = 49664

        F_xa = 0
        F_h0 = 8192
        F_h1 = 24576
        F_wf = 40960
        F_wo = 45056
        F_w0b = 49152
        F_w1b = 51200
        F_g = 53248
        F_mb = 54272
        F_mb1 = 54784

        A_stg = 55296

        pb = [Res(en(nc.psum_tensor(f"pb{i}", [128, 512], F32))) for i in range(8)]

        block = en(nc.Block())

        ALLC = tuple([cS, cG, cPE, cACT, cDVE, cR, cW_, cWhg, cKR, cXA, cAS, cXS,
                      cKS, cBC, cIC] + cH + cWF + cWO + cW0B + cW1B + cHR + cOS)
        R_row = Res()
        R = dict((k, Res()) for k in (
            "xn", "warena", "h2a", "h2b", "hball", "hstreamA", "hstreamB",
            "ckv", "cq", "whg0", "whg1", "K", "V", "Q", "P0", "P1", "atn", "ksq", "kraw",
            "xa", "h0", "h1", "mbc", "stgA", "stgB", "gA", "gB",
            "wfA", "wfB", "woA", "woB", "w0bA", "w0bB", "w1bA", "w1bB",
            "hresA", "hresB", "outA", "outB", "rbcA", "rbcB"))

        def mmgroup(bank, mks, extra_first=()):
            n = len(mks)
            for i, mk in enumerate(mks):
                waits = ()
                if i == 0:
                    waits = list(bank.w_waits()) + list(extra_first)
                prog.pe(mk(i == 0, i == n - 1), waits=waits,
                        inc=(cPE if i == n - 1 else None))
            bank.set_ready(cPE)

        prog.dve(lambda e: e.memset(ones_cb[:], 1.0))
        prog.dve(lambda e: e.memset(ones_rb[:], 1.0))
        prog.dve(lambda e: e.memset(ones_sq[:], 1.0))
        prog.dve(lambda e: e.memset(ones_rf[:], 1.0), inc=cDVE)
        v_ones = cDVE.n

        prog.dma_g(lambda e: e.dma_start(cos_b[:], cosT[:, :]), ctr=cIC)
        prog.dma_g(lambda e: e.dma_start(sin_b[:], sinT[:, :]), ctr=cIC)
        for nm in bc:
            def mk(nm=nm):
                return lambda e: e.dma_start(bc[nm][:], bcin[nm][:, :])
            prog.dma_s(mk(), ctr=cBC)

        ra = rows[:, 0:2048]
        rb2 = rows[:, 2048:4096]
        prog.dma_s(lambda e: e.dma_start(ra, Wg_in[0:1, :]), ctr=cR)
        prog.dma_s(lambda e: e.dma_start(rb2, Wg_in[1:2, :]), ctr=cR)
        v = cR.n
        prog.dve(lambda e: e.tensor_tensor(ra, ra, rb2, ALU.subtract),
                 waits=[(cR, v)], inc=cDVE)
        R_row.add_read(cDVE)
        v = cDVE.n
        prog.dve(lambda e: e.tensor_reduce(sumw[:, 0:1], ra, AX, ALU.add), inc=cDVE)
        prog.dma_s(lambda e: e.dma_start(wdiff_dram.rearrange("(a d) -> a d", a=1), ra),
                   waits=[(cDVE, v)] + R_row.w_waits(), ctr=cR)
        v = cR.n
        prog.dma_g(lambda e: e.dma_start(wdiff_col[:],
                                         wdiff_dram.rearrange("(c p) -> p c", p=128),
                                         allow_slow_non_contiguous=True),
                   waits=[(cR, v)], ctr=cIC)
        v_wcol = cIC.n
        R_row.add_read(cIC)

        prog.dma_s(lambda e: e.dma_start(ra, b_o_r[:, :]), waits=R_row.w_waits(), ctr=cR)
        prog.dma_s(lambda e: e.dma_start(rb2, b1b_r[:, :]), ctr=cR)
        v = cR.n
        prog.dve(lambda e: e.tensor_tensor(row_sum_bf[:], ra, rb2, ALU.add),
                 waits=[(cR, v)], inc=cDVE)
        R_row.add_read(cDVE)
        v = cDVE.n
        prog.dma_s(lambda e: e.dma_start(ra, b0b_r[:, :]),
                   waits=[(cDVE, v)] + R_row.w_waits(), ctr=cR)
        v = cR.n
        prog.dve(lambda e: e.tensor_tensor(row_dif_bf[:], ra, rb2, ALU.subtract),
                 waits=[(cR, v)], inc=cDVE)
        R_row.add_read(cDVE)

        wc_done = {}
        wc_all = 0
        conv_jobs = []
        stgv = [bfv(A_stg, FF)]
        stgR = [R["stgA"]]
        CvOUK, CvOUV, CvOUQ, CvOQR = 0, 256, 768, 1024
        if do_attn:
            for hgj in range(HGN):
                for ccj in range(CCH):
                    loads = []
                    for off, wsrc, w0, w1 in ((CvOUK, W_uk, hgj * 256, 256),
                                              (CvOUV, W_uv, hgj * 512, 512),
                                              (CvOUQ, W_uq, hgj * 256, 256),
                                              (CvOQR, W_qr, hgj * 256, 256)):
                        def mk_l(off=off, wsrc=wsrc, w0=w0, w1=w1, ccj=ccj):
                            return lambda e: e.dma_start(
                                stgv[0][:, off:off + w1],
                                wsrc[ccj * 128:(ccj + 1) * 128, w0:w0 + w1])
                        loads.append(mk_l())
                    def mk_s(hgj=hgj, ccj=ccj):
                        return lambda e: e.dma_start(
                            wattn_img[hgj, :, ccj * 1280:(ccj + 1) * 1280],
                            stgv[0][:, 0:1280])
                    conv_jobs.append(("wattn", loads, mk_s(),
                                      hgj == HGN - 1 and ccj == CCH - 1))
        if do_ffn:
            for key, src4, dst4, nblk, per in (
                    ("W0a", W0a4, W0a4b, FCH, DCH * 128),
                    ("W1a", W1a4, W1a4b, FCH, DCH * 128),
                    ("Wo", Wo4, Wo4b, DCH, DCH * 128),
                    ("W0b", W0b4, W0b4b, DCH, FCH * 128),
                    ("W1b", W1b4, W1b4b, DCH, FCH * 128)):
                for i in range(nblk):
                    def mk_l(i=i, src4=src4, per=per):
                        return lambda e: e.dma_start(stgv[0][:, 0:per],
                                                     src4[i].rearrange("p c f -> p (c f)"))
                    def mk_s(i=i, dst4=dst4, per=per):
                        return lambda e: e.dma_start(dst4[i].rearrange("p c f -> p (c f)"),
                                                     stgv[0][:, 0:per])
                    conv_jobs.append((key, [mk_l()], mk_s(), i == nblk - 1))

        def pump(k):
            nonlocal wc_all
            while k > 0 and conv_jobs:
                key, loads, store, last = conv_jobs.pop(0)
                for mk_l in loads:
                    prog.dma_g(mk_l, waits=stgR[0].w_waits(), ctr=cGc)
                stgR[0].set_ready(cGc)
                prog.dma_s(store, waits=stgR[0].r_wait(), ctr=cSc)
                stgR[0].add_read(cSc)
                if last:
                    wc_done[key] = cSc.n
                wc_all = cSc.n
                k -= 1

        tree_guard = [None, None]

        for s in range(n_seq):
            prog.join(ALLC)

            xn_tb = bfv(A_xn, 8192)
            wdkv_v = bfv(A_w, 8192)
            wdq_v = bfv(A_w + 8192, 8192)
            wkr_v = bfv(A_w + 16384, 16384)
            hst = [f32v(0, 512), f32v(512, 512)]
            hstR = [R["hstreamA"], R["hstreamB"]]
            h2 = [bfv(A_h2, 512), bfv(A_h2 + 512, 512)]
            h2R = [R["h2a"], R["h2b"]]
            hbf_all = bfv(43008, 8192)
            xtmp = f32v(1024, 512)
            rbv = f32v(1536, 512)
            mrbv = f32v(2048, 512)
            mean_r = rows[:, 0:512]
            msq_r = rows[:, 512:1024]
            var_r = rows[:, 1024:1536]
            rstd_r = rows[:, 1536:2048]
            mrb_r = rows[:, 2048:2560]
            gt_r = rows[:, 2560:3072]

            wv = R["warena"].w_waits()
            firstw = True
            for (w_v, wsrc, wid) in ((wdkv_v, W_dkv, DKV), (wdq_v, W_dq, DKV),
                                     (wkr_v, W_kr, 1024)):
                for dc in range(DCH):
                    def mk(dc=dc, w_v=w_v, wsrc=wsrc, wid=wid):
                        return lambda e: e.dma_start(w_v[:, dc * wid:(dc + 1) * wid],
                                                     wsrc[dc * 128:(dc + 1) * 128, :])
                    prog.dma_g(mk(), waits=(wv if firstw else ()), ctr=cW_)
                    firstw = False
            R["warena"].set_ready(cW_)
            v_w = cW_.n

            for tb in range(TBN):
                for dc in range(DCH):
                    b = dc % 2
                    def mk_h(dc=dc, b=b, s=s, tb=tb):
                        return lambda e: e.dma_start(hst[b][:], hT4[s, tb, dc])
                    prog.dma_s(mk_h(), waits=hstR[b].w_waits(), ctr=cH[b])
                    hstR[b].set_ready(cH[b])
                    def mk_sq(b=b):
                        return lambda e: e.activation(h2[b][:], hst[b][:], AF.Square)
                    prog.act(mk_sq(), waits=list(hstR[b].r_wait()) + h2R[b].w_waits(), inc=cACT)
                    h2R[b].set_ready(cACT)
                    def mk_cp(b=b, dc=dc):
                        return lambda e: e.tensor_copy(hbf_all[:, dc * TW:(dc + 1) * TW],
                                                       hst[b][:])
                    prog.dve(mk_cp(), waits=list(hstR[b].r_wait()) +
                             (R["hball"].w_waits() if dc == 0 else []), inc=cDVE)
                    v_cp = cDVE.n
                    st, sp = (dc == 0), (dc == DCH - 1)
                    w5 = pb[5].w_waits() if st else []
                    w6 = pb[6].w_waits() if st else []
                    w7 = pb[7].w_waits() if st else []
                    def mk_mean(dc=dc, st=st, sp=sp):
                        return lambda e: e.matmul(pb[5].t[0:1, :], ones_cb[:],
                                                  hbf_all[:, dc * TW:(dc + 1) * TW],
                                                  start=st, stop=sp)
                    prog.pe(mk_mean(), waits=w5 + [(cDVE, v_cp)], inc=cPE)
                    R["hball"].add_read(cPE)
                    def mk_msq(b=b, st=st, sp=sp):
                        return lambda e: e.matmul(pb[6].t[0:1, :], ones_cb[:], h2[b][:],
                                                  start=st, stop=sp)
                    prog.pe(mk_msq(), waits=w6 + list(h2R[b].r_wait()), inc=cPE)
                    h2R[b].add_read(cPE)
                    def mk_gt(dc=dc, b=b, st=st, sp=sp):
                        return lambda e: e.matmul(pb[7].t[0:1, :], wdiff_col[:, dc:dc + 1],
                                                  hst[b][:], start=st, stop=sp)
                    prog.pe(mk_gt(), waits=(w7 + [(cIC, v_wcol)]) if st else [], inc=cPE)
                    hstR[b].add_read(cPE)
                for b_ in (pb[5], pb[6], pb[7]):
                    b_.set_ready(cPE)
                prog.dve(lambda e: e.tensor_scalar(mean_r, pb[5].t[0:1, :], 1.0 / D, None, ALU.mult),
                         waits=pb[5].r_wait(), inc=cDVE)
                pb[5].add_read(cDVE)
                prog.dve(lambda e: e.tensor_scalar(msq_r, pb[6].t[0:1, :], 1.0 / D, None, ALU.mult),
                         waits=pb[6].r_wait(), inc=cDVE)
                pb[6].add_read(cDVE)
                prog.dve(lambda e: e.tensor_tensor(var_r, mean_r, mean_r, ALU.mult))
                prog.dve(lambda e: e.tensor_tensor(var_r, msq_r, var_r, ALU.subtract), inc=cDVE)
                prog.dve(lambda e: e.tensor_scalar(var_r, var_r, 1e-5, None, ALU.add), inc=cDVE)
                v = cDVE.n
                prog.act(lambda e: e.activation(var_r, var_r, AF.Ln), waits=[(cDVE, v)])
                prog.act(lambda e: e.activation(rstd_r, var_r, AF.Exp, scale=-0.5), inc=cACT)
                v_rstd = cACT.n
                prog.dve(lambda e: e.tensor_tensor(mrb_r, mean_r, rstd_r, ALU.mult),
                         waits=[(cACT, v_rstd)])
                prog.dve(lambda e: e.tensor_scalar(gt_r, mean_r, sumw[:, 0:1], None, ALU.mult))
                prog.dve(lambda e: e.tensor_tensor(gt_r, pb[7].t[0:1, :], gt_r, ALU.subtract),
                         waits=pb[7].r_wait())
                prog.dve(lambda e: e.tensor_tensor(gt_r, gt_r, rstd_r, ALU.mult))
                def mk_m(tb=tb):
                    return lambda e: e.tensor_scalar(m_row_bf[:, tb * TW:(tb + 1) * TW], gt_r,
                                                     0.0, None, ALU.is_ge)
                prog.dve(mk_m(), inc=cDVE)
                pb[7].add_read(cDVE)
                v_rows = cDVE.n
                def mk_b1(st, sp):
                    return lambda e: e.matmul(pb[4].t[:], ones_rf[:], rstd_r, start=True, stop=True)
                mmgroup(pb[4], [mk_b1], extra_first=[(cDVE, v_rows), (cDVE, v_ones)])
                prog.dve(lambda e: e.tensor_copy(rbv[:], pb[4].t[:]), waits=pb[4].r_wait(), inc=cDVE)
                pb[4].add_read(cDVE)
                def mk_b2(st, sp):
                    return lambda e: e.matmul(pb[4].t[:], ones_rf[:], mrb_r, start=True, stop=True)
                mmgroup(pb[4], [mk_b2])
                prog.dve(lambda e: e.tensor_copy(mrbv[:], pb[4].t[:]), waits=pb[4].r_wait(), inc=cDVE)
                pb[4].add_read(cDVE)
                v_bc = cDVE.n
                wx = R["xn"].w_waits()
                for dc in range(DCH):
                    def mk_a(dc=dc):
                        return lambda e: e.tensor_tensor(xtmp[:],
                                                         hbf_all[:, dc * TW:(dc + 1) * TW],
                                                         rbv[:], ALU.mult)
                    prog.dve(mk_a(), waits=[(cDVE, v_bc)] + (wx if dc == 0 else []))
                    def mk_b(dc=dc):
                        return lambda e: e.tensor_tensor(xn_tb[:, dc * TW:(dc + 1) * TW], xtmp[:],
                                                         mrbv[:], ALU.subtract)
                    prog.dve(mk_b(), inc=cDVE)
                    v_xn = cDVE.n
                    def mk_d(dc=dc, s=s, tb=tb):
                        return lambda e: e.dma_start(xn_dram[s, tb, dc],
                                                     xn_tb[:, dc * TW:(dc + 1) * TW])
                    prog.dma_s(mk_d(), waits=[(cDVE, v_xn)], ctr=cXS)
                R["xn"].set_ready(cDVE)
                R["xn"].add_read(cXS)
                R["hball"].add_read(cDVE)
                R["hball"].set_ready(cDVE)
                v_apply = cDVE.n

                for (w_v, bnm, dst, Rdst) in ((wdkv_v, "dkv", ckv_sb, R["ckv"]),
                                              (wdq_v, "dq", cq_sb, R["cq"])):
                    wdst = Rdst.w_waits() if tb == 0 else []
                    for cc in range(CCH):
                        bk = pb[cc % 2]
                        mks = []
                        for dc in range(DCH):
                            def mk_mm(cc=cc, dc=dc, bk=bk, w_v=w_v):
                                return lambda st, sp: (lambda e: e.matmul(
                                    bk.t[:], w_v[:, dc * DKV + cc * 128: dc * DKV + (cc + 1) * 128],
                                    xn_tb[:, dc * TW:(dc + 1) * TW], start=st, stop=sp))
                            mks.append(mk_mm())
                        mmgroup(bk, mks, extra_first=[(cW_, v_w), (cDVE, v_apply)] +
                                (wdst if cc == 0 else []))
                        R["xn"].add_read(cPE)
                        R["warena"].add_read(cPE)
                        def mk_ev(cc=cc, tb=tb, bk=bk, dst=dst, bnm=bnm):
                            return lambda e: e.activation(
                                dst[:, cc * S + tb * TW: cc * S + (tb + 1) * TW], bk.t[:],
                                AF.Identity, bias=bc[bnm][:, cc:cc + 1], scale=1.0)
                        prog.act(mk_ev(), waits=bk.r_wait(), inc=cACT)
                        bk.add_read(cACT)
                    if tb == TBN - 1:
                        Rdst.set_ready(cACT)

                for fc in range(8):
                    bk = pb[fc % 2]
                    mks = []
                    for dc in range(DCH):
                        def mk_mm(fc=fc, dc=dc, bk=bk):
                            return lambda st, sp: (lambda e: e.matmul(
                                bk.t[:], wkr_v[:, dc * 1024 + fc * 128: dc * 1024 + (fc + 1) * 128],
                                xn_tb[:, dc * TW:(dc + 1) * TW], start=st, stop=sp))
                        mks.append(mk_mm())
                    mmgroup(bk, mks, extra_first=[(cW_, v_w), (cDVE, v_apply)])
                    R["xn"].add_read(cPE)
                    R["warena"].add_read(cPE)
                    def mk_ev(fc=fc, bk=bk):
                        return lambda e: e.activation(kraw[:], bk.t[:], AF.Identity,
                                                      bias=bc["kr"][:, fc:fc + 1], scale=1.0)
                    prog.act(mk_ev(), waits=list(bk.r_wait()) + R["kraw"].w_waits(), inc=cACT)
                    bk.add_read(cACT)
                    R["kraw"].set_ready(cACT)
                    vk = cACT.n
                    for gi, (d0, s0, sc) in enumerate(((0, 32, -1.0), (32, 0, 1.0),
                                                       (64, 96, -1.0), (96, 64, 1.0))):
                        def mk_sw(d0=d0, s0=s0, sc=sc):
                            return lambda e: e.tensor_scalar(krot[d0:d0 + 32, :],
                                                             kraw[s0:s0 + 32, :], sc, None, ALU.mult)
                        prog.dve(mk_sw(), waits=([(cACT, vk)] if gi == 0 else ()))
                    def mk_t3(tb=tb):
                        return lambda e: e.tensor_tensor(t3_sb[:], kraw[:],
                                                         cos_b[:, tb * TW:(tb + 1) * TW], ALU.mult)
                    prog.dve(mk_t3())
                    def mk_t4(tb=tb):
                        return lambda e: e.tensor_tensor(t4_sb[:], krot[:],
                                                         sin_b[:, tb * TW:(tb + 1) * TW], ALU.mult)
                    prog.dve(mk_t4(), inc=cDVE)
                    R["kraw"].add_read(cDVE)
                    prog.dve(lambda e: e.tensor_tensor(t3_sb[:], t3_sb[:], t4_sb[:], ALU.add),
                             inc=cDVE)
                    v_t = cDVE.n
                    def mk_st(fc=fc, tb=tb):
                        return lambda e: e.dma_start(kr_dram[fc, :, tb * TW:(tb + 1) * TW], t3_sb[:])
                    prog.dma_s(mk_st(), waits=[(cDVE, v_t)], ctr=cKS)
                if s == 0 and do_ffn:
                    pump(14)
            xn_spill = cXS.n
            kr_spill = cKS.n
            if "gate" in dbg:
                prog.join(ALLC)
                def mk_g(s=s):
                    return lambda e: e.dma_start(dbg_aps["gate"][s], m_row_bf[:, :])
                prog.dma_s(mk_g(), ctr=cS)
                def mk_g2(s=s):
                    return lambda e: e.dma_start(dbg_aps["grow"][s], rows[:, :])
                prog.dma_s(mk_g2(), ctr=cS)
                prog.join(ALLC)
            if "xn" in dbg:
                prog.join(ALLC)
                for tb in range(TBN):
                    for dc in range(DCH):
                        def mk_l(s=s, tb=tb, dc=dc):
                            return lambda e: e.dma_start(kraw[:], xn_dram[s, tb, dc])
                        prog.dma_g(mk_l(), waits=[(cDVE, cDVE.n), (cS, cS.n)], ctr=cG)
                        v = cG.n
                        prog.dve(lambda e: e.tensor_copy(xtmp[:], kraw[:]), waits=[(cG, v)], inc=cDVE)
                        v = cDVE.n
                        def mk_s2(s=s, tb=tb, dc=dc):
                            return lambda e: e.dma_start(dbg_aps["xn"][s, tb, dc], xtmp[:])
                        prog.dma_s(mk_s2(), waits=[(cDVE, v)], ctr=cS)
                prog.join(ALLC)
            if "kr" in dbg:
                prog.join(ALLC)
                for fc in range(8):
                    for tb in range(TBN):
                        def mk_l(fc=fc, tb=tb):
                            return lambda e: e.dma_start(kraw[:], kr_dram[fc, :, tb * TW:(tb + 1) * TW])
                        prog.dma_g(mk_l(), waits=[(cDVE, cDVE.n), (cS, cS.n)], ctr=cG)
                        v = cG.n
                        prog.dve(lambda e: e.tensor_copy(xtmp[:], kraw[:]), waits=[(cG, v)], inc=cDVE)
                        v = cDVE.n
                        def mk_s2(fc=fc, tb=tb):
                            return lambda e: e.dma_start(dbg_aps["kr"][fc, :, tb * TW:(tb + 1) * TW],
                                                         xtmp[:])
                        prog.dma_s(mk_s2(), waits=[(cDVE, v)], ctr=cS)
                prog.join(ALLC)
            if "ckv" in dbg or "cq" in dbg:
                prog.join(ALLC)
                for (dkey, dst) in (("ckv", ckv_sb), ("cq", cq_sb)):
                    if dkey not in dbg:
                        continue
                    for cc in range(CCH):
                        for tb in range(TBN):
                            def mk_c(cc=cc, tb=tb, dst=dst):
                                return lambda e: e.tensor_copy(
                                    xtmp[:], dst[:, cc * S + tb * TW:cc * S + (tb + 1) * TW])
                            prog.dve(mk_c(), waits=[(cS, cS.n)], inc=cDVE)
                            v = cDVE.n
                            def mk_s2(cc=cc, tb=tb, dkey=dkey):
                                return lambda e: e.dma_start(
                                    dbg_aps[dkey][cc, :, tb * TW:(tb + 1) * TW], xtmp[:])
                            prog.dma_s(mk_s2(), waits=[(cDVE, v)], ctr=cS)
                prog.join(ALLC)

            if not do_attn:
                continue

            K_sb = bfv(T_K, HGS * S)
            V_sb = bfv(T_V, DCH * TW)
            Q_sb = bfv(T_Q, HGS * TW)
            P_v = [bfv(T_P0, DCH * TW), bfv(T_P1, DCH * TW)]
            PR = [R["P0"], R["P1"]]
            atn_sb = bfv(T_atn, HGS * TW)
            whgv = [bfv(T_whg, CCH * 1280), bfv(44032, CCH * 1280)]
            tree = bfv(T_tree, 4 * TW)
            ksq_sb = bfv(T_ksq, 2048)
            rbcv = [f32v(0, 512), f32v(1536, 512)]
            rbcR = [R["rbcA"], R["rbcB"]]
            buv_v = f32v(512, 512)
            xtmp_a = f32v(1024, 512)
            den = [rows[:, 0:512], rows[:, 1024:1536]]
            rec = [rows[:, 512:1024], rows[:, 1536:2048]]
            SC128 = float(1.0 / np.sqrt(128.0))

            def emit_scores(h4, hg, qb, k_act, k_dma, negc_v, qw=()):
                par = h4 % 2
                Pb, PbR = P_v[par], PR[par]
                wP = PbR.w_waits()
                ed = {}
                for kc in range(DCH):
                    Sb = pb[kc % 4]
                    def mk(kc=kc, h4=h4, Sb=Sb):
                        return lambda e: e.matmul(
                            Sb.t[:],
                            K_sb[:, h4 * S + kc * 128: h4 * S + (kc + 1) * 128],
                            Q_sb[:, h4 * TW:(h4 + 1) * TW], start=True, stop=True)
                    waits = list(Sb.w_waits())
                    if kc == 0:
                        waits += [(cACT, k_act), (cKR, k_dma)] + list(qw)
                    prog.pe(mk(), waits=waits, inc=cPE)
                    Sb.set_ready(cPE)
                    def mk_ex(kc=kc, h4=h4, Sb=Sb, Pb=Pb):
                        return lambda e: e.activation(
                            Pb[:, kc * TW:(kc + 1) * TW], Sb.t[:], AF.Exp,
                            bias=negc_sb[:, h4:h4 + 1], scale=SC128)
                    prog.act(mk_ex(), waits=list(Sb.r_wait()) +
                             ([(cDVE, negc_v)] + wP if kc == 0 else []),
                             inc=cACT)
                    Sb.add_read(cACT)
                    ed[kc // 2] = cACT.n
                PbR.set_ready(cACT)
                return ed

            def emit_fin1(h4, ed):
                par = h4 % 2
                Pb, PbR = P_v[par], PR[par]
                u = pb[6 + par]
                acc = [tree[:, (2 * par) * TW:(2 * par + 1) * TW],
                       tree[:, (2 * par + 1) * TW:(2 * par + 2) * TW]]
                w0 = [(cACT, ed[0])] + ([tree_guard[par]] if tree_guard[par] else [])
                def mk_a0(Pb=Pb, acc=acc):
                    return lambda e: e.tensor_tensor(acc[0], Pb[:, 0:TW],
                                                     Pb[:, TW:2 * TW], ALU.add)
                prog.dve(mk_a0(), waits=w0)
                for i in range(1, 15):
                    kc = i + 1
                    def mk_ai(i=i, kc=kc, Pb=Pb, acc=acc):
                        return lambda e: e.tensor_tensor(acc[i % 2], acc[(i - 1) % 2],
                                                         Pb[:, kc * TW:(kc + 1) * TW],
                                                         ALU.add)
                    prog.dve(mk_ai(), waits=[(cACT, ed[kc // 2])],
                             inc=(cDVE if i == 14 else None))
                v_tree = cDVE.n
                troot = acc[0]
                PbR.add_read(cDVE)
                def mk_dn(st, sp):
                    def f(e, troot=troot, u=u):
                        return e.matmul(u.t[:], ones_sq[:], troot[:],
                                        start=True, stop=True)
                    return f
                mmgroup(u, [mk_dn], extra_first=[(cDVE, v_tree)])
                tree_guard[par] = (cPE, cPE.n)
                def mk_ln(u=u):
                    return lambda e: e.activation(xtmp_a[:], u.t[:], AF.Ln)
                prog.act(mk_ln(), waits=u.r_wait())
                def mk_exr(par=par):
                    return lambda e: e.activation(rbcv[par][:], xtmp_a[:], AF.Exp,
                                                  scale=-1.0)
                prog.act(mk_exr(), waits=rbcR[par].w_waits(), inc=cACT)
                u.add_read(cACT)
                rbcR[par].set_ready(cACT)

            def emit_fin2(h4, ed, v_V):
                par = h4 % 2
                Pb, PbR = P_v[par], PR[par]
                obk = pb[4 + par]
                mks = []
                for kc in range(DCH):
                    def mk_pv(kc=kc, h4=h4, obk=obk, Pb=Pb):
                        return lambda st, sp: (lambda e: e.matmul(
                            obk.t[:], V_sb[:, kc * TW + h4 * 128: kc * TW + (h4 + 1) * 128],
                            Pb[:, kc * TW:(kc + 1) * TW], start=st, stop=sp))
                    mks.append(mk_pv())
                mmgroup(obk, mks, extra_first=[(cACT, ed[7]), (cDVE, v_V)])
                R["V"].add_read(cPE)
                PbR.add_read(cPE)
                R["K"].add_read(cPE)
                R["Q"].add_read(cPE)
                watn = R["atn"].w_waits() if h4 == 0 else []
                def mk_oev(h4=h4, obk=obk, par=par):
                    return lambda e: e.tensor_tensor(atn_sb[:, h4 * TW:(h4 + 1) * TW],
                                                     obk.t[:], rbcv[par][:], ALU.mult)
                prog.dve(mk_oev(), waits=list(obk.r_wait()) + list(rbcR[par].r_wait()) + watn,
                         inc=cDVE)
                obk.add_read(cDVE)
                rbcR[par].add_read(cDVE)

            OUK, OUV, OUQ, OQR = 0, 256, 768, 1024

            def load_whg(hg):
                Rw = R["whg%d" % (hg % 2)]
                wv = list(Rw.w_waits()) + [(cSc, wc_done["wattn"])]
                dst = whgv[hg % 2]
                def mk(hg=hg, dst=dst):
                    return lambda e: e.dma_start(dst[:], wattn_img[hg])
                prog.dma_s(mk(), waits=wv, ctr=cWhg)
                Rw.set_ready(cWhg)
                return cWhg.n

            while conv_jobs and conv_jobs[0][0] == "wattn":
                pump(1)
            prog.join(ALLC)
            v_whg_tab = {0: load_whg(0)}
            for hg in range(n_hg):
                whg = whgv[hg % 2]
                Rwhg = R["whg%d" % (hg % 2)]
                v_whg = v_whg_tab[hg]
                vb = prog.dma_s(lambda e: e.dma_start(rb2, b_uv_r[:, :]),
                                waits=R_row.w_waits(), ctr=cR)
                def mk_bv(st, sp):
                    def f(e, hg=hg):
                        return e.matmul(pb[6].t[:], ones_rf[:], rb2[:, hg * TW:(hg + 1) * TW],
                                        start=True, stop=True)
                    return f
                mmgroup(pb[6], [mk_bv], extra_first=[(cR, vb), (cDVE, v_ones)])
                R_row.add_read(cPE)
                prog.dve(lambda e: e.tensor_copy(buv_v[:], pb[6].t[:]), waits=pb[6].r_wait(),
                         inc=cDVE)
                pb[6].add_read(cDVE)
                v_buv = cDVE.n

                wK = R["K"].w_waits()
                for fc2 in range(2):
                    for tb in range(TBN):
                        bk = pb[4 + (fc2 * TBN + tb) % 2]
                        mks = []
                        for cc in range(CCH):
                            def mk_mm(fc2=fc2, tb=tb, cc=cc, bk=bk, whg=whg):
                                return lambda st, sp: (lambda e: e.matmul(
                                    bk.t[:],
                                    whg[:, cc * 1280 + OUK + fc2 * 128: cc * 1280 + OUK + (fc2 + 1) * 128],
                                    ckv_sb[:, cc * S + tb * TW: cc * S + (tb + 1) * TW],
                                    start=st, stop=sp))
                            mks.append(mk_mm())
                        mmgroup(bk, mks, extra_first=[(cWhg, v_whg)] + list(R["ckv"].r_wait()) +
                                ((wK + [(cXS, xn_spill)]) if fc2 == 0 and tb == 0 else []))
                        R["ckv"].add_read(cPE)
                        Rwhg.add_read(cPE)
                        for hh in range(2):
                            h4 = fc2 * 2 + hh
                            habs = hg * HGS + h4
                            bcol = (habs * 64) // 128
                            brow = (habs * 64) % 128
                            def mk_ev(tb=tb, hh=hh, h4=h4, bcol=bcol, brow=brow, bk=bk):
                                return lambda e: e.activation(
                                    K_sb[0:64, h4 * S + tb * TW: h4 * S + (tb + 1) * TW],
                                    bk.t[hh * 64:(hh + 1) * 64, :], AF.Identity,
                                    bias=bc["uk"][brow:brow + 64, bcol:bcol + 1], scale=1.0)
                            prog.act(mk_ev(), waits=(bk.r_wait() if hh == 0 else ()),
                                     inc=(cACT if hh == 1 else None))
                        bk.add_read(cACT)
                for h4 in range(HGS):
                    habs = hg * HGS + h4
                    fc = habs // 2
                    row = (habs % 2) * 64
                    def mk_l(h4=h4, fc=fc, row=row):
                        return lambda e: e.dma_start(K_sb[64:128, h4 * S:(h4 + 1) * S],
                                                     kr_dram[fc, row:row + 64, :])
                    prog.dma_s(mk_l(), waits=([(cKS, kr_spill)] if h4 == 0 else ()), ctr=cKR)
                R["K"].set_ready(cKR)
                k_act = cACT.n
                k_dma = cKR.n

                for h4 in range(HGS):
                    habs = hg * HGS + h4
                    u = pb[6 + h4 % 2]
                    def mk_sq(h4=h4):
                        return lambda e: e.activation(ksq_sb[:], K_sb[:, h4 * S:(h4 + 1) * S],
                                                      AF.Square)
                    prog.act(mk_sq(), waits=[(cACT, k_act), (cKR, k_dma)] + R["ksq"].w_waits(),
                             inc=cACT)
                    R["ksq"].set_ready(cACT)
                    v_ksq = cACT.n
                    for tb in range(TBN):
                        def mk_mm(tb=tb, u=u):
                            return lambda st, sp: (lambda e: e.matmul(
                                u.t[0:1, :], ones_cb[:], ksq_sb[:, tb * TW:(tb + 1) * TW],
                                start=True, stop=True))
                        mmgroup(u, [mk_mm()], extra_first=[(cACT, v_ksq)])
                        R["ksq"].add_read(cPE)
                        def mk_rd(tb=tb, u=u):
                            return lambda e: e.tensor_reduce(sc11[:, 8 + tb:9 + tb], u.t[0:1, :],
                                                             AX, ALU.max)
                        prog.dve(mk_rd(), waits=u.r_wait(), inc=cDVE)
                        u.add_read(cDVE)
                    prog.dve(lambda e: e.tensor_tensor(sc11[:, 8:9], sc11[:, 8:9], sc11[:, 9:10], ALU.max))
                    prog.dve(lambda e: e.tensor_tensor(sc11[:, 10:11], sc11[:, 10:11], sc11[:, 11:12], ALU.max))
                    prog.dve(lambda e: e.tensor_tensor(sc11[:, 8:9], sc11[:, 8:9], sc11[:, 10:11], ALU.max),
                             inc=cDVE)
                    v = cDVE.n
                    def mk_ln(habs=habs):
                        return lambda e: e.activation(klog[:, habs:habs + 1], sc11[:, 8:9], AF.Ln)
                    prog.act(mk_ln(), waits=[(cDVE, v)], inc=cACT)
                    v = cACT.n
                    def mk_adj(habs=habs):
                        return lambda e: e.tensor_scalar(klog[:, habs:habs + 1],
                                                         klog[:, habs:habs + 1], 0.5, NLN128,
                                                         ALU.mult, ALU.add)
                    prog.dve(mk_adj(), waits=[(cACT, v)], inc=cDVE)
                klog_dve = cDVE.n

                wV = R["V"].w_waits()
                for tt in range(DCH):
                    bk = pb[4 + tt % 2]
                    mks = []
                    for cc in range(CCH):
                        def mk_mm(tt=tt, cc=cc, bk=bk, whg=whg):
                            return lambda st, sp: (lambda e: e.matmul(
                                bk.t[:], ckv_sb[:, cc * S + tt * 128: cc * S + (tt + 1) * 128],
                                whg[:, cc * 1280 + OUV: cc * 1280 + OUV + 512], start=st, stop=sp))
                        mks.append(mk_mm())
                    mmgroup(bk, mks, extra_first=[(cWhg, v_whg), (cDVE, v_buv)] + (wV if tt == 0 else []))
                    R["ckv"].add_read(cPE)
                    Rwhg.add_read(cPE)
                    def mk_ev(tt=tt, bk=bk):
                        return lambda e: e.tensor_tensor(V_sb[:, tt * TW:(tt + 1) * TW], bk.t[:],
                                                         buv_v[:], ALU.add)
                    prog.dve(mk_ev(), waits=bk.r_wait(), inc=cDVE)
                    bk.add_read(cDVE)
                R["V"].set_ready(cDVE)
                v_V = cDVE.n
                if hg + 1 < n_hg:
                    v_whg_tab[hg + 1] = load_whg(hg + 1)

                for qb in range(n_qb):
                    wQ = R["Q"].w_waits()
                    for part, off in (("base", OUQ), ("rot", OQR)):
                        for fc2 in range(2):
                            bk = pb[4 + fc2 % 2]
                            mks = []
                            for cc in range(CCH):
                                def mk_mm(off=off, fc2=fc2, cc=cc, qb=qb, bk=bk, whg=whg):
                                    return lambda st, sp: (lambda e: e.matmul(
                                        bk.t[:],
                                        whg[:, cc * 1280 + off + fc2 * 128: cc * 1280 + off + (fc2 + 1) * 128],
                                        cq_sb[:, cc * S + qb * TW: cc * S + (qb + 1) * TW],
                                        start=st, stop=sp))
                                mks.append(mk_mm())
                            mmgroup(bk, mks, extra_first=[(cWhg, v_whg)] + list(R["cq"].r_wait()) +
                                    (wQ if part == "base" and fc2 == 0 else []))
                            R["cq"].add_read(cPE)
                            Rwhg.add_read(cPE)
                            if part == "base":
                                for hh in range(2):
                                    h4 = fc2 * 2 + hh
                                    habs = hg * HGS + h4
                                    bcol = (habs * 64) // 128
                                    brow = (habs * 64) % 128
                                    def mk_ev(hh=hh, h4=h4, bcol=bcol, brow=brow, bk=bk):
                                        return lambda e: e.activation(
                                            Q_sb[0:64, h4 * TW:(h4 + 1) * TW],
                                            bk.t[hh * 64:(hh + 1) * 64, :], AF.Identity,
                                            bias=bc["uq"][brow:brow + 64, bcol:bcol + 1], scale=1.0)
                                    prog.act(mk_ev(), waits=(bk.r_wait() if hh == 0 else ()),
                                             inc=(cACT if hh == 1 else None))
                                bk.add_read(cACT)
                            else:
                                fcg = hg * 2 + fc2
                                def mk_ev(fcg=fcg, bk=bk):
                                    return lambda e: e.activation(kraw[:], bk.t[:], AF.Identity,
                                                                  bias=bc["qr"][:, fcg:fcg + 1],
                                                                  scale=1.0)
                                prog.act(mk_ev(), waits=list(bk.r_wait()) + R["kraw"].w_waits(),
                                         inc=cACT)
                                bk.add_read(cACT)
                                R["kraw"].set_ready(cACT)
                                vk = cACT.n
                                for gi, (d0, s0, sc) in enumerate(((0, 32, -1.0), (32, 0, 1.0),
                                                                   (64, 96, -1.0), (96, 64, 1.0))):
                                    def mk_sw(d0=d0, s0=s0, sc=sc):
                                        return lambda e: e.tensor_scalar(
                                            krot[d0:d0 + 32, :], kraw[s0:s0 + 32, :], sc, None, ALU.mult)
                                    prog.dve(mk_sw(), waits=([(cACT, vk)] if gi == 0 else ()))
                                def mk_t3(qb=qb):
                                    return lambda e: e.tensor_tensor(
                                        t3_sb[:], kraw[:], cos_b[:, qb * TW:(qb + 1) * TW], ALU.mult)
                                prog.dve(mk_t3())
                                def mk_t4(qb=qb):
                                    return lambda e: e.tensor_tensor(
                                        t4_sb[:], krot[:], sin_b[:, qb * TW:(qb + 1) * TW], ALU.mult)
                                prog.dve(mk_t4(), inc=cDVE)
                                R["kraw"].add_read(cDVE)
                                for hh in range(2):
                                    h4 = fc2 * 2 + hh
                                    def mk_add(h4=h4, hh=hh):
                                        return lambda e: e.tensor_tensor(
                                            Q_sb[64:128, h4 * TW:(h4 + 1) * TW],
                                            t3_sb[hh * 64:(hh + 1) * 64, :],
                                            t4_sb[hh * 64:(hh + 1) * 64, :], ALU.add)
                                    prog.dve(mk_add(), inc=(cDVE if hh == 1 else None))
                    R["Q"].set_ready(cDVE)
                    q_act = cACT.n
                    q_dve = cDVE.n

                    prog.act(lambda e: e.activation(ksq_sb[:], Q_sb[:, 0:2048], AF.Square),
                             waits=[(cACT, q_act), (cDVE, q_dve)] + R["ksq"].w_waits(), inc=cACT)
                    R["ksq"].set_ready(cACT)
                    v_sq = cACT.n
                    negc_done = {}
                    for h4 in range(HGS):
                        u = pb[6 + h4 % 2]
                        def mk_qn(h4=h4, u=u):
                            return lambda st, sp: (lambda e: e.matmul(
                                u.t[0:1, :], ones_cb[:], ksq_sb[:, h4 * TW:(h4 + 1) * TW],
                                start=True, stop=True))
                        mmgroup(u, [mk_qn()], extra_first=[(cACT, v_sq)])
                        R["ksq"].add_read(cPE)
                        def mk_rd(h4=h4, u=u):
                            return lambda e: e.tensor_reduce(sc11[:, h4:h4 + 1], u.t[0:1, :],
                                                             AX, ALU.max)
                        prog.dve(mk_rd(), waits=u.r_wait(), inc=cDVE)
                        u.add_read(cDVE)
                    v = cDVE.n
                    h0c = hg * HGS
                    prog.act(lambda e: e.activation(sc11[:, 4:8], sc11[:, 0:4], AF.Ln),
                             waits=[(cDVE, v), (cDVE, klog_dve)], inc=cACT)
                    v = cACT.n
                    prog.dve(lambda e: e.tensor_scalar(sc11[:, 12:16], sc11[:, 4:8], 0.5,
                                                       None, ALU.mult), waits=[(cACT, v)])
                    def mk_comb(h0c=h0c):
                        return lambda e: e.tensor_tensor(sc11[:, 4:8], sc11[:, 12:16],
                                                         klog[:, h0c:h0c + 4], ALU.add)
                    prog.dve(mk_comb(), inc=cDVE)
                    v = cDVE.n
                    prog.act(lambda e: e.activation(sc11[:, 4:8], sc11[:, 4:8], AF.Exp),
                             waits=[(cDVE, v)], inc=cACT)
                    v = cACT.n
                    prog.dve(lambda e: e.tensor_scalar(sc11[:, 4:8], sc11[:, 4:8], -1.0,
                                                       SHIFT, ALU.mult, ALU.add),
                             waits=[(cACT, v)], inc=cDVE)
                    v = cDVE.n
                    for h4 in range(HGS):
                        u = pb[6 + h4 % 2]
                        def mk_nb(st, sp):
                            def f(e, h4=h4, u=u):
                                return e.matmul(u.t[:, 0:1], ones_rf[:], sc11[:, 4 + h4:5 + h4],
                                                start=True, stop=True)
                            return f
                        mmgroup(u, [mk_nb], extra_first=[(cDVE, v)])
                        def mk_cpn(h4=h4, u=u):
                            return lambda e: e.tensor_copy(negc_sb[:, h4:h4 + 1], u.t[:, 0:1])
                        prog.dve(mk_cpn(), waits=u.r_wait(), inc=cDVE)
                        u.add_read(cDVE)
                    for h4 in range(HGS):
                        negc_done[h4] = cDVE.n

                    eds = {}
                    for h4 in range(HGS):
                        eds[h4] = emit_scores(h4, hg, qb, k_act, k_dma, negc_done[h4],
                                              qw=[(cACT, q_act), (cDVE, q_dve)])
                        if h4 >= 1:
                            emit_fin1(h4 - 1, eds[h4 - 1])
                            emit_fin2(h4 - 1, eds[h4 - 1], v_V)
                    emit_fin1(HGS - 1, eds[HGS - 1])
                    emit_fin2(HGS - 1, eds[HGS - 1], v_V)

                    R["atn"].set_ready(cDVE)
                    for h4 in range(HGS):
                        def mk_st(s=s, qb=qb, hg=hg, h4=h4):
                            return lambda e: e.dma_start(attn_dram[s, qb, hg * HGS + h4],
                                                         atn_sb[:, h4 * TW:(h4 + 1) * TW])
                        prog.dma_s(mk_st(), waits=(R["atn"].r_wait() if h4 == 0 else ()), ctr=cAS)
                    R["atn"].add_read(cAS)
                if s == 0 and do_ffn:
                    pump(14)
            if do_ffn:
                pump(len(conv_jobs))
            attn_spill = cAS.n

            if "attn" in dbg:
                prog.join(ALLC)
                for qb in range(TBN):
                    for dc in range(DCH):
                        def mk_l(s=s, qb=qb, dc=dc):
                            return lambda e: e.dma_start(kraw[:], attn_dram[s, qb, dc])
                        prog.dma_g(mk_l(), waits=[(cDVE, cDVE.n), (cS, cS.n)], ctr=cG)
                        v = cG.n
                        prog.dve(lambda e: e.tensor_copy(xtmp_a[:], kraw[:]), waits=[(cG, v)],
                                 inc=cDVE)
                        v = cDVE.n
                        def mk_s2(s=s, qb=qb, dc=dc):
                            return lambda e: e.dma_start(dbg_aps["attn"][s, qb, dc], xtmp_a[:])
                        prog.dma_s(mk_s2(), waits=[(cDVE, v)], ctr=cS)
                prog.join(ALLC)

            if not do_ffn:
                continue

            if do_ffn:
                pump(len(conv_jobs))
            prog.join(ALLC)
            if "sent" in dbg:
                prog.dve(lambda e: e.memset(bfv(F_wf, 4096)[:], 111.0))
                prog.dve(lambda e: e.memset(bfv(F_wo, 4096)[:], 222.0))
                prog.dve(lambda e: e.memset(bfv(F_w0b, 2048)[:], 333.0))
                prog.dve(lambda e: e.memset(bfv(F_w1b, 2048)[:], 444.0))
                prog.dve(lambda e: e.memset(bfv(F_xa, 8192)[:], 555.0))
                prog.dve(lambda e: e.memset(bfv(F_h0, 16384)[:], 0.001))
                prog.dve(lambda e: e.memset(bfv(F_h1, 16384)[:], 0.002), inc=cDVE)
                prog.join(ALLC)
            xa_v = bfv(F_xa, 8192)
            h0m = bfv(F_h0, FCH * TW)
            h1m = bfv(F_h1, FCH * TW)
            wfv = [bfv(F_wf, 2048), bfv(F_wf + 2048, 2048)]
            wov = [bfv(F_wo, 2048), bfv(F_wo + 2048, 2048)]
            w0v = [bfv(F_w0b, 1024), bfv(F_w0b + 1024, 1024)]
            w1v = [bfv(F_w1b, 1024), bfv(F_w1b + 1024, 1024)]
            gv = [bfv(F_g, TW), bfv(F_g + TW, TW)]
            m_bc = bfv(F_mb, TW)
            m1_bc = bfv(F_mb1, TW)
            hresv = [f32v(0, 512), f32v(512, 512)]
            outv = [f32v(1024, 512), f32v(1536, 512)]
            wfR = [R["wfA"], R["wfB"]]
            woR = [R["woA"], R["woB"]]
            w0R = [R["w0bA"], R["w0bB"]]
            w1R = [R["w1bA"], R["w1bB"]]
            gR = [R["gA"], R["gB"]]
            hrR = [R["hresA"], R["hresB"]]
            outR = [R["outA"], R["outB"]]

            for qb in range(n_qb):
                wx = R["xa"].w_waits()
                for dc in range(DCH):
                    def mk(dc=dc, s=s, qb=qb):
                        return lambda e: e.dma_start(xa_v[:, dc * TW:(dc + 1) * TW],
                                                     xn_dram[s, qb, dc])
                    prog.dma_s(mk(), waits=(wx + [(cXS, xn_spill), (cSc, wc_all)]
                                            if dc == 0 else ()), ctr=cXA)
                R["xa"].set_ready(cXA)
                v_x = cXA.n

                def mk_mb(st, sp):
                    def f(e, qb=qb):
                        return e.matmul(pb[6].t[:], ones_rb[:, 0:128],
                                        m_row_bf[:, qb * TW:(qb + 1) * TW], start=True, stop=True)
                    return f
                mmgroup(pb[6], [mk_mb], extra_first=R["mbc"].w_waits())
                prog.dve(lambda e: e.tensor_copy(m_bc[:], pb[6].t[:]), waits=pb[6].r_wait())
                prog.dve(lambda e: e.tensor_scalar(m1_bc[:], pb[6].t[:], -1.0, 1.0,
                                                   ALU.mult, ALU.add), inc=cDVE)
                pb[6].add_read(cDVE)
                R["mbc"].set_ready(cDVE)
                v_masks = cDVE.n

                for (w4b, wdone, bnm, func, hdst, Rh) in (
                        (W0a4b, wc_done.get("W0a", 0), "a0", AF.Gelu, h0m, R["h0"]),
                        (W1a4b, wc_done.get("W1a", 0), "a1", AF.Relu, h1m, R["h1"])):
                    wh = Rh.w_waits()
                    for fc in range(FCH):
                        b2 = fc % 2
                        def mk_lw(fc=fc, b2=b2, w4b=w4b):
                            return lambda e: e.dma_start(wfv[b2][:],
                                                         w4b[fc].rearrange("p c f -> p (c f)"))
                        prog.dma_s(mk_lw(), waits=[(cSc, wc_all)] + wfR[b2].w_waits(), ctr=cWF[b2])
                        wfR[b2].set_ready(cWF[b2])
                        v_w = cWF[b2].n
                        bk = pb[2 + b2]
                        mks = []
                        for dc in range(DCH):
                            def mk_mm(dc=dc, b2=b2, bk=bk):
                                return lambda st, sp: (lambda e: e.matmul(
                                    bk.t[:], wfv[b2][:, dc * 128:(dc + 1) * 128],
                                    xa_v[:, dc * TW:(dc + 1) * TW], start=st, stop=sp))
                            mks.append(mk_mm())
                        mmgroup(bk, mks, extra_first=[(cWF[b2], v_w), (cXA, v_x)] + (wh if fc == 0 else []))
                        R["xa"].add_read(cPE)
                        wfR[b2].add_read(cPE)
                        def mk_ga(fc=fc, b2=b2, bk=bk, bnm=bnm, func=func):
                            return lambda e: e.activation(gv[b2][:], bk.t[:], func,
                                                          bias=bc[bnm][:, fc:fc + 1], scale=1.0)
                        prog.act(mk_ga(), waits=list(bk.r_wait()) + gR[b2].w_waits(), inc=cACT)
                        bk.add_read(cACT)
                        gR[b2].set_ready(cACT)
                        mb = m_bc if func == AF.Gelu else m1_bc
                        def mk_mk(fc=fc, b2=b2, hdst=hdst, mb=mb):
                            return lambda e: e.tensor_tensor(hdst[:, fc * TW:(fc + 1) * TW],
                                                             gv[b2][:], mb[:], ALU.mult)
                        prog.dve(mk_mk(), waits=list(gR[b2].r_wait()) + [(cDVE, v_masks)], inc=cDVE)
                        gR[b2].add_read(cDVE)
                        R["mbc"].add_read(cDVE)
                    Rh.set_ready(cDVE)
                h_done = cDVE.n
                if "hm" in dbg and qb == 0:
                    for fc in range(FCH):
                        def mk_d0(s=s, fc=fc):
                            return lambda e: e.dma_start(dbg_aps["h0d"][s, fc], h0m[:, fc * TW:(fc + 1) * TW])
                        prog.dma_s(mk_d0(), ctr=cS)
                        def mk_d1(s=s, fc=fc):
                            return lambda e: e.dma_start(dbg_aps["h1d"][s, fc], h1m[:, fc * TW:(fc + 1) * TW])
                        prog.dma_s(mk_d1(), ctr=cS)
                    prog.join(ALLC)
                if "ffn" in dbg and s == 0 and qb == 0:
                    prog.join(ALLC)
                    prog.dma_s(lambda e: e.dma_start(dbg_aps["h0m"][:, :], h0m[:, :]), ctr=cS)
                    prog.dma_s(lambda e: e.dma_start(dbg_aps["h1m"][:, :], h1m[:, :]), ctr=cS)
                    prog.dma_s(lambda e: e.dma_start(dbg_aps["mbc"][:, 0:TW], m_bc[:]), ctr=cS)
                    prog.dma_s(lambda e: e.dma_start(dbg_aps["mbc"][:, TW:2 * TW], m1_bc[:]), ctr=cS)
                    prog.dma_s(lambda e: e.dma_start(dbg_aps["xa"][:, :], xa_v[:, :]), ctr=cS)
                    prog.join(ALLC)

                wx = R["xa"].w_waits()
                for dc in range(DCH):
                    def mk(dc=dc, s=s, qb=qb):
                        return lambda e: e.dma_start(xa_v[:, dc * TW:(dc + 1) * TW],
                                                     attn_dram[s, qb, dc])
                    prog.dma_s(mk(), waits=(wx + [(cAS, attn_spill)] if dc == 0 else ()), ctr=cXA)
                R["xa"].set_ready(cXA)
                v_a = cXA.n

                for do_ in range(DCH):
                    b2 = do_ % 2
                    def mk_lo(do_=do_, b2=b2):
                        return lambda e: e.dma_start(wov[b2][:],
                                                     Wo4b[do_].rearrange("p c f -> p (c f)"))
                    prog.dma_s(mk_lo(), waits=[(cSc, wc_all)] + woR[b2].w_waits(),
                               ctr=cWO[b2])
                    woR[b2].set_ready(cWO[b2])
                    v_wo = cWO[b2].n
                    hb = hresv[b2]
                    def mk_lr(do_=do_, hb=hb, s=s, qb=qb):
                        return lambda e: e.dma_start(hb[:], hT4[s, qb, do_])
                    prog.dma_s(mk_lr(), waits=hrR[b2].w_waits(), ctr=cHR[b2])
                    hrR[b2].set_ready(cHR[b2])
                    v_r = cHR[b2].n

                    bk = pb[2 + b2]
                    mks0 = []
                    for kc in range(DCH):
                        def mk_mm(kc=kc, b2=b2, bk=bk):
                            return lambda e: e.matmul(
                                bk.t[:], wov[b2][:, kc * 128:(kc + 1) * 128],
                                xa_v[:, kc * TW:(kc + 1) * TW], start=(kc == 0), stop=False)
                        mks0.append(mk_mm())
                    for i, mk in enumerate(mks0):
                        waits = []
                        if i == 0:
                            waits = bk.w_waits() + [(cWO[b2], v_wo), (cXA, v_a), (cDVE, h_done)]
                        prog.pe(mk, waits=waits, inc=(cPE if i == DCH - 1 else None))
                    R["xa"].add_read(cPE)
                    woR[b2].add_read(cPE)
                    for (hsrc, wvv, wRr, w4b, cPc) in ((h0m, w0v, w0R, W0b4b, cW0B),
                                                       (h1m, w1v, w1R, W1b4b, cW1B)):
                        for pc in range(4):
                            p2 = pc % 2
                            def mk_lp(do_=do_, pc=pc, p2=p2, wvv=wvv, w4b=w4b):
                                return lambda e: e.dma_start(
                                    wvv[p2][:],
                                    w4b[do_, :, pc * 8:(pc + 1) * 8, :].rearrange("p c f -> p (c f)"))
                            prog.dma_s(mk_lp(), waits=[(cSc, wc_all)] + wRr[p2].w_waits(),
                                       ctr=cPc[p2])
                            wRr[p2].set_ready(cPc[p2])
                            v_wp = cPc[p2].n
                            for j in range(8):
                                fc = pc * 8 + j
                                def mk_mm(fc=fc, j=j, p2=p2, hsrc=hsrc, wvv=wvv, bk=bk):
                                    return lambda e: e.matmul(
                                        bk.t[:], wvv[p2][:, j * 128:(j + 1) * 128],
                                        hsrc[:, fc * TW:(fc + 1) * TW], start=False, stop=False)
                                prog.pe(mk_mm(), waits=([(cPc[p2], v_wp)] if j == 0 else []),
                                        inc=(cPE if j == 7 else None))
                            wRr[p2].add_read(cPE)
                    def mk_r1(do_=do_, bk=bk):
                        return lambda e: e.matmul(bk.t[:], row_sum_bf[:, do_ * 128:(do_ + 1) * 128],
                                                  ones_rb[:], start=False, stop=False)
                    prog.pe(mk_r1())
                    def mk_r2(do_=do_, qb=qb, bk=bk):
                        return lambda e: e.matmul(bk.t[:], row_dif_bf[:, do_ * 128:(do_ + 1) * 128],
                                                  m_row_bf[:, qb * TW:(qb + 1) * TW],
                                                  start=False, stop=True)
                    prog.pe(mk_r2(), inc=cPE)
                    bk.set_ready(cPE)
                    R["h0"].add_read(cPE)
                    R["h1"].add_read(cPE)
                    if "ostep" in dbg and qb == 0 and do_ == 0:
                        v_g = cPE.n
                        def mk_d1(s=s):
                            return lambda e: e.dma_start(dbg_aps["wov"][s], wov[0][:])
                        prog.dma_s(mk_d1(), waits=[(cPE, v_g)], ctr=cS)
                        def mk_d2(s=s):
                            return lambda e: e.dma_start(dbg_aps["w0v"][s, :, 0:1024], w0v[0][:])
                        prog.dma_s(mk_d2(), ctr=cS)
                        def mk_d3(s=s):
                            return lambda e: e.dma_start(dbg_aps["w0v"][s, :, 1024:2048], w0v[1][:])
                        prog.dma_s(mk_d3(), ctr=cS)
                        def mk_d4(s=s):
                            return lambda e: e.dma_start(dbg_aps["w1v"][s, :, 0:1024], w1v[0][:])
                        prog.dma_s(mk_d4(), ctr=cS)
                        def mk_d5(s=s):
                            return lambda e: e.dma_start(dbg_aps["w1v"][s, :, 1024:2048], w1v[1][:])
                        prog.dma_s(mk_d5(), ctr=cS)
                        v_d = cS.n
                        prog.emit("tensor", None, waits=[(cS, v_d)])
                    def mk_ev(b2=b2, bk=bk, hb=hb):
                        return lambda e: e.tensor_tensor(outv[b2][:], bk.t[:], hb[:], ALU.add)
                    prog.dve(mk_ev(), waits=list(bk.r_wait()) + [(cHR[b2], v_r)] +
                             list(hrR[b2].r_wait()) + outR[b2].w_waits(), inc=cDVE)
                    bk.add_read(cDVE)
                    hrR[b2].add_read(cDVE)
                    outR[b2].set_ready(cDVE)
                    def mk_so(s=s, qb=qb, do_=do_, b2=b2):
                        return lambda e: e.dma_start(outT[s, qb, do_], outv[b2][:])
                    prog.dma_s(mk_so(), waits=outR[b2].r_wait(), ctr=cOS[b2])
                    outR[b2].add_read(cOS[b2])

        if "dump" in dbg:
            prog.join(ALLC)
            for sd in range(n_seq):
                for tb in range(TBN):
                    for dc in range(DCH):
                        def mk1(sd=sd, tb=tb, dc=dc):
                            return lambda e: e.dma_start(dbg_aps["xnd"][sd, tb, dc],
                                                         xn_dram[sd, tb, dc])
                        prog.dma_g(mk1(), ctr=cG)
                        def mk2(sd=sd, tb=tb, dc=dc):
                            return lambda e: e.dma_start(dbg_aps["attnd"][sd, tb, dc],
                                                         attn_dram[sd, tb, dc])
                        prog.dma_g(mk2(), ctr=cG)
        finalw = [(c, c.n) for c in ALLC + (cSc, cGc)]
        prog.emit("sync", None, waits=finalw)
        prog.emit("gpsimd", None, waits=finalw)

        prog.replay(block)

    return nc



def _col(b):
    b = np.asarray(b, np.float32)
    return np.ascontiguousarray(b.reshape(-1, 128).T)


def _prep_inputs(ins):
    h = np.asarray(ins["L0_attn_h"], np.float32)
    f32 = lambda x: np.ascontiguousarray(np.asarray(x, np.float32))

    W0aT = np.ascontiguousarray(f32(ins["L0_ffn_W0a"]).T)
    W1aT = np.ascontiguousarray(f32(ins["L0_ffn_W1a"]).T)
    W0bT = np.ascontiguousarray(f32(ins["L0_ffn_W0b"]).T)
    W1bT = np.ascontiguousarray(f32(ins["L0_ffn_W1b"]).T)
    Wo = f32(ins["L0_attn_W_o"])

    W0a4 = np.ascontiguousarray(W0aT.reshape(DCH, 128, FCH, 128).transpose(2, 1, 0, 3))
    W1a4 = np.ascontiguousarray(W1aT.reshape(DCH, 128, FCH, 128).transpose(2, 1, 0, 3))
    W0b4 = np.ascontiguousarray(W0bT.reshape(FCH, 128, DCH, 128).transpose(2, 1, 0, 3))
    W1b4 = np.ascontiguousarray(W1bT.reshape(FCH, 128, DCH, 128).transpose(2, 1, 0, 3))
    Wo4 = np.ascontiguousarray(Wo.reshape(DCH, 128, DCH, 128).transpose(2, 1, 0, 3))

    d_rope = 64
    inv_freq = (1.0 / (10000.0 ** (np.arange(0, d_rope, 2, dtype=np.float64) / d_rope)))
    t = np.arange(S, dtype=np.float64)
    freqs = t[:, None] * inv_freq[None, :]
    emb = np.concatenate([freqs, freqs], axis=-1)
    cosT = np.ascontiguousarray(np.tile(np.cos(emb).T.astype(np.float32), (2, 1)))
    sinT = np.ascontiguousarray(np.tile(np.sin(emb).T.astype(np.float32), (2, 1)))

    rowf = lambda b: np.ascontiguousarray(np.asarray(b, np.float32).reshape(1, -1))

    shared = dict(
        W_dkv=f32(ins["L0_attn_W_dkv"]), W_dq=f32(ins["L0_attn_W_dq"]),
        W_uk=f32(ins["L0_attn_W_uk"]), W_uv=f32(ins["L0_attn_W_uv"]),
        W_uq=f32(ins["L0_attn_W_uq"]), W_qr=f32(ins["L0_attn_W_qr"]),
        W_kr=f32(ins["L0_attn_W_kr"]),
        W0a4=W0a4, W1a4=W1a4, W0b4=W0b4, W1b4=W1b4, Wo4=Wo4,
        bc_dkv=_col(ins["L0_attn_b_dkv"]), bc_dq=_col(ins["L0_attn_b_dq"]),
        bc_uk=_col(ins["L0_attn_b_uk"]), bc_uq=_col(ins["L0_attn_b_uq"]),
        bc_qr=_col(ins["L0_attn_b_qr"]), bc_kr=_col(ins["L0_attn_b_kr"]),
        bc_a0=_col(ins["L0_ffn_b0a"]), bc_a1=_col(ins["L0_ffn_b1a"]),
        b_uv_r=rowf(ins["L0_attn_b_uv"]), b_o_r=rowf(ins["L0_attn_b_o"]),
        b0b_r=rowf(ins["L0_ffn_b0b"]), b1b_r=rowf(ins["L0_ffn_b1b"]),
        Wg=f32(ins["L0_ffn_Wg"]), cosT=cosT, sinT=sinT,
    )

    in_maps = []
    for c in range(8):
        hc = h[2 * c:2 * c + 2]
        hT = hc.transpose(0, 2, 1)
        hT4 = np.ascontiguousarray(hT.reshape(2, DCH, 128, TBN, TW).transpose(0, 3, 1, 2, 4))
        m = dict(shared)
        m["hT4"] = hT4
        in_maps.append(m)
    return in_maps


def _assemble_output(results):
    outs = []
    for c in range(8):
        o = results[c]["outT"]
        o = o.transpose(0, 1, 4, 2, 3).reshape(2, S, D)
        outs.append(o)
    return np.ascontiguousarray(np.concatenate(outs, axis=0)).astype(np.float32)


def kernel(**inputs):
    in_maps = _prep_inputs(inputs)
    nc = build_program()
    res = run_bass_kernel_spmd(nc, in_maps, core_ids=list(range(8)))
    return _assemble_output(res.results)


# version 17
# speedup vs baseline: 1.1731x; 1.1731x over previous
import numpy as np
from contextlib import ExitStack

import concourse.bass as bass
import concourse.mybir as mybir
from concourse.bass_utils import run_bass_kernel_spmd
from concourse.alu_op_type import AluOpType as ALU

F32 = mybir.dt.float32
BF16 = mybir.dt.bfloat16
AF = mybir.ActivationFunctionType
AX = mybir.AxisListType.X

D = 2048
S = 2048
NSEQ = 2
DCH = 16
TBN = 4
TW = 512
HEADS = 16
HGN = 4
HGS = 4
DKV = 512
CCH = 4
FF = 4096
FCH = 32
SHIFT = 30.0
NLN128 = -float(np.log(128.0) * 0.5)

ENGINES = ("sync", "tensor", "vector", "scalar", "gpsimd")


class Ctr:
    def __init__(self, handle, name):
        self.h = handle
        self.name = name
        self.n = 0


class Prog:
    def __init__(self):
        self.ops = {e: [] for e in ENGINES}
        self._lastwait = {e: {} for e in ENGINES}

    def emit(self, eng, fn, waits=(), inc=None, inc_n=1):
        lw = self._lastwait[eng]
        wl = []
        for w in waits:
            if w is None:
                continue
            c, v = w
            if v > 0 and v > lw.get(c.name, 0):
                wl.append((c, v))
                lw[c.name] = v
        self.ops[eng].append((tuple(wl), fn, inc, inc_n))
        if inc is not None:
            inc.n += inc_n
        return inc.n if inc is not None else None

    def pe(self, fn, waits=(), inc=None):
        return self.emit("tensor", fn, waits, inc, 1)

    def dve(self, fn, waits=(), inc=None):
        return self.emit("vector", fn, waits, inc, 1)

    def act(self, fn, waits=(), inc=None):
        return self.emit("scalar", fn, waits, inc, 1)

    def dma_s(self, fn, waits=(), ctr=None):
        return self.emit("sync", fn, waits, ctr, 16)

    def dma_g(self, fn, waits=(), ctr=None):
        return self.emit("gpsimd", fn, waits, ctr, 16)

    def join(self, ctrs):
        snap = [(c, c.n) for c in ctrs]
        for e in ENGINES:
            for (c, v) in snap:
                if v > 0 and v > self._lastwait[e].get(c.name, 0):
                    self.ops[e].append((((c, v),), None, None, 0))
                    self._lastwait[e][c.name] = v

    def replay(self, block):
        def mk(eng_name):
            def body(eng):
                for wl, fn, inc, n in self.ops[eng_name]:
                    for (c, v) in wl:
                        eng.wait_ge(c.h, v)
                    if fn is None:
                        continue
                    inst = fn(eng)
                    if inc is not None:
                        inst.then_inc(inc.h, n)
            return body
        block.sync(mk("sync"))
        block.tensor(mk("tensor"))
        block.vector(mk("vector"))
        block.scalar(mk("scalar"))
        block.gpsimd(mk("gpsimd"))


class Res:

    def __init__(self, t=None):
        self.t = t
        self.ready = None
        self.rguards = []

    def w_waits(self):
        w = list(self.rguards)
        if self.ready:
            w.append(self.ready)
        self.rguards = []
        return w

    def set_ready(self, ctr):
        self.ready = (ctr, ctr.n)

    def r_wait(self):
        return [self.ready] if self.ready else []

    def add_read(self, ctr):
        self.rguards.append((ctr, ctr.n))


def build_program(n_seq=NSEQ, n_hg=HGN, n_qb=TBN, do_attn=True, do_ffn=True, dbg=()):
    nc = bass.Bass("TRN2", target_bir_lowering=False)
    dbg = set(dbg)

    AP = lambda name, shape, dt, kind: nc.dram_tensor(name, shape, dt, kind=kind).ap()
    I, O, N_ = "ExternalInput", "ExternalOutput", "Internal"

    hT4 = AP("hT4", [NSEQ, TBN, DCH, 128, TW], F32, I)
    wattn_img = AP("wattn_img", [HGN, 128, 4 * 1280], mybir.dt.bfloat16, N_)
    W_dkv = AP("W_dkv", [D, DKV], F32, I)
    W_dq = AP("W_dq", [D, DKV], F32, I)
    W_uk = AP("W_uk", [DKV, 1024], F32, I)
    W_uv = AP("W_uv", [DKV, 2048], F32, I)
    W_uq = AP("W_uq", [DKV, 1024], F32, I)
    W_qr = AP("W_qr", [DKV, 1024], F32, I)
    W_kr = AP("W_kr", [D, 1024], F32, I)
    W0a4 = AP("W0a4", [FCH, 128, DCH, 128], F32, I)
    W1a4 = AP("W1a4", [FCH, 128, DCH, 128], F32, I)
    W0b4 = AP("W0b4", [DCH, 128, FCH, 128], F32, I)
    W1b4 = AP("W1b4", [DCH, 128, FCH, 128], F32, I)
    Wo4 = AP("Wo4", [DCH, 128, DCH, 128], F32, I)
    bcin = {nm: AP("bc_" + nm, [128, f // 128], F32, I) for nm, f in (
        ("dkv", DKV), ("dq", DKV), ("uk", 1024), ("uq", 1024), ("qr", 1024),
        ("kr", 1024), ("a0", FF), ("a1", FF))}
    b_uv_r = AP("b_uv_r", [1, 2048], F32, I)
    b_o_r = AP("b_o_r", [1, D], F32, I)
    b0b_r = AP("b0b_r", [1, D], F32, I)
    b1b_r = AP("b1b_r", [1, D], F32, I)
    Wg_in = AP("Wg", [2, D], F32, I)
    cosT = AP("cosT", [128, S], F32, I)
    sinT = AP("sinT", [128, S], F32, I)

    outT = AP("outT", [NSEQ, TBN, DCH, 128, TW], F32, O)

    xn_dram = AP("xn_dram", [NSEQ, TBN, DCH, 128, TW], BF16, N_)
    attn_dram = AP("attn_dram", [NSEQ, TBN, DCH, 128, TW], BF16, N_)
    kr_dram = AP("kr_dram", [8, 128, S], BF16, N_)
    wdiff_dram = AP("wdiff_dram", [D], F32, N_)
    W0a4b = AP("W0a4b", [FCH, 128, DCH, 128], BF16, N_)
    W1a4b = AP("W1a4b", [FCH, 128, DCH, 128], BF16, N_)
    W0b4b = AP("W0b4b", [DCH, 128, FCH, 128], BF16, N_)
    W1b4b = AP("W1b4b", [DCH, 128, FCH, 128], BF16, N_)
    Wo4b = AP("Wo4b", [DCH, 128, DCH, 128], BF16, N_)

    dbg_aps = {}
    if "ckv" in dbg:
        dbg_aps["ckv"] = AP("dbg_ckv", [CCH, 128, S], F32, O)
    if "cq" in dbg:
        dbg_aps["cq"] = AP("dbg_cq", [CCH, 128, S], F32, O)
    if "kr" in dbg:
        dbg_aps["kr"] = AP("dbg_kr", [8, 128, S], F32, O)
    if "xn" in dbg:
        dbg_aps["xn"] = AP("dbg_xn", [NSEQ, TBN, DCH, 128, TW], F32, O)
    if "attn" in dbg:
        dbg_aps["attn"] = AP("dbg_attn", [NSEQ, TBN, DCH, 128, TW], F32, O)
    if "qk" in dbg:
        dbg_aps["K"] = AP("dbg_K", [128, HGS * S], mybir.dt.bfloat16, O)
        dbg_aps["Q"] = AP("dbg_Q", [128, HGS * TW], mybir.dt.bfloat16, O)
        dbg_aps["P"] = AP("dbg_P", [128, DCH * TW], mybir.dt.bfloat16, O)
        dbg_aps["klog"] = AP("dbg_klog", [1, HEADS], F32, O)
        dbg_aps["negc"] = AP("dbg_negc", [128, HGS], F32, O)
        dbg_aps["den"] = AP("dbg_den", [1, 4096], F32, O)
    if "gate" in dbg:
        dbg_aps["gate"] = AP("dbg_gate", [NSEQ, 1, S], mybir.dt.bfloat16, O)
        dbg_aps["grow"] = AP("dbg_grow", [NSEQ, 1, 4096], F32, O)
    if "dump" in dbg:
        dbg_aps["xnd"] = AP("dbg_xnd", [NSEQ, TBN, DCH, 128, TW], mybir.dt.bfloat16, O)
        dbg_aps["attnd"] = AP("dbg_attnd", [NSEQ, TBN, DCH, 128, TW], mybir.dt.bfloat16, O)
    if "hm" in dbg:
        dbg_aps["h0d"] = AP("dbg_h0d", [NSEQ, FCH, 128, TW], mybir.dt.bfloat16, O)
        dbg_aps["h1d"] = AP("dbg_h1d", [NSEQ, FCH, 128, TW], mybir.dt.bfloat16, O)
    if "ostep" in dbg:
        dbg_aps["wov"] = AP("dbg_wov", [NSEQ, 128, 2048], mybir.dt.bfloat16, O)
        dbg_aps["w0v"] = AP("dbg_w0v", [NSEQ, 128, 2048], mybir.dt.bfloat16, O)
        dbg_aps["w1v"] = AP("dbg_w1v", [NSEQ, 128, 2048], mybir.dt.bfloat16, O)
    if "ffn" in dbg:
        dbg_aps["h0m"] = AP("dbg_h0m", [128, FCH * TW], mybir.dt.bfloat16, O)
        dbg_aps["h1m"] = AP("dbg_h1m", [128, FCH * TW], mybir.dt.bfloat16, O)
        dbg_aps["mbc"] = AP("dbg_mbc", [128, 2 * TW], mybir.dt.bfloat16, O)
        dbg_aps["xa"] = AP("dbg_xa", [128, DCH * TW], mybir.dt.bfloat16, O)

    prog = Prog()

    with ExitStack() as stack:
        en = stack.enter_context

        cS = Ctr(en(nc.semaphore("cS")), "cS")
        cG = Ctr(en(nc.semaphore("cG")), "cG")
        cPE = Ctr(en(nc.semaphore("cPE")), "cPE")
        cACT = Ctr(en(nc.semaphore("cACT")), "cACT")
        cDVE = Ctr(en(nc.semaphore("cDVE")), "cDVE")
        cSc = Ctr(en(nc.semaphore("cSc")), "cSc")
        cGc = Ctr(en(nc.semaphore("cGc")), "cGc")
        cR = Ctr(en(nc.semaphore("cR")), "cR")
        mkc = lambda nm: Ctr(en(nc.semaphore(nm)), nm)
        cH = [mkc("cH0"), mkc("cH1")]
        cW_ = mkc("cW")
        cWhg = mkc("cWhg")
        cKR = mkc("cKR")
        cXA = mkc("cXA")
        cWF = [mkc("cWF0"), mkc("cWF1")]
        cWO = [mkc("cWO0"), mkc("cWO1")]
        cW0B = [mkc("cW0B0"), mkc("cW0B1")]
        cW1B = [mkc("cW1B0"), mkc("cW1B1")]
        cHR = [mkc("cHR0"), mkc("cHR1")]
        cOS = [mkc("cOS0"), mkc("cOS1")]
        cAS = mkc("cAS")
        cXS = mkc("cXS")
        cKS = mkc("cKS")
        cBC = mkc("cBC")
        cIC = mkc("cIC")
        ALLC = None

        sb = lambda name, shape, dt: en(nc.sbuf_tensor(name, shape, dt))

        cos_b = sb("cos_b", [128, S], BF16)
        sin_b = sb("sin_b", [128, S], BF16)
        ones_cb = sb("ones_cb", [128, 1], BF16)
        ones_sq = sb("ones_sq", [128, 128], BF16)
        ones_rb = sb("ones_rb", [1, TW], BF16)
        ones_rf = sb("ones_rf", [1, 128], F32)
        wdiff_col = sb("wdiff_col", [128, DCH], F32)
        bc = {}
        for nm, f in (("dkv", DKV), ("dq", DKV), ("uk", 1024), ("uq", 1024),
                      ("qr", 1024), ("kr", 1024), ("a0", FF), ("a1", FF)):
            bc[nm] = sb("bcs_" + nm, [128, f // 128], F32)
        negc_sb = sb("negc_sb", [128, HGS], F32)
        kraw = sb("kraw", [128, TW], BF16)
        krot = sb("krot", [128, TW], BF16)
        t3_sb = sb("t3_sb", [128, TW], BF16)
        t4_sb = sb("t4_sb", [128, TW], BF16)
        rows = sb("rows", [1, 4096], F32)
        row_sum_bf = sb("row_sum_bf", [1, D], BF16)
        row_dif_bf = sb("row_dif_bf", [1, D], BF16)
        m_row_bf = sb("m_row_bf", [1, S], BF16)
        klog = sb("klog", [1, HEADS], F32)
        sumw = sb("sumw", [1, 1], F32)
        sc11 = sb("sc11", [1, 16], F32)

        ckv_sb = sb("ckv_sb", [128, CCH * S], BF16)
        cq_sb = sb("cq_sb", [128, CCH * S], BF16)

        ABF_N = 62976
        abf = sb("abf", [128, ABF_N], BF16)
        AF32_N = 2560
        af32 = sb("af32", [128, AF32_N], F32)

        bfv = lambda o, n: abf[:, o:o + n]
        f32v = lambda o, n: af32[:, o:o + n]

        A_xn = 0
        A_w = 8192
        A_h2 = 40960
        A_hbf = 41984

        T_K = 0
        T_V = 8192
        T_Q = 16384
        T_P0 = 18432
        T_P1 = 26624
        T_atn = 34816
        T_whg = 36864
        T_tree = 41984
        T_ksq = 49664

        F_xa = 0
        F_h0 = 8192
        F_h1 = 24576
        F_wf = 40960
        F_wo = 45056
        F_w0b = 49152
        F_g = 53248
        F_mb = 54272
        F_wf2 = 54784
        F_w1b = 58880

        pb = [Res(en(nc.psum_tensor(f"pb{i}", [128, 512], F32))) for i in range(8)]

        block = en(nc.Block())

        ALLC = tuple([cS, cG, cPE, cACT, cDVE, cR, cW_, cWhg, cKR, cXA, cAS, cXS,
                      cKS, cBC, cIC] + cH + cWF + cWO + cW0B + cW1B + cHR + cOS)
        R_row = Res()
        R = dict((k, Res()) for k in (
            "xn", "warena", "h2a", "h2b", "hball", "hstreamA", "hstreamB",
            "ckv", "cq", "whg0", "whg1", "K", "V", "Q", "P0", "P1", "atn", "ksq", "kraw",
            "xa", "h0", "h1", "mbc", "stgA", "stgB", "gA", "gB",
            "wfA", "wfB", "wfC", "wfD", "woA", "woB", "w0bA", "w0bB", "w1bA", "w1bB",
            "hresA", "hresB", "outA", "outB", "rbcA", "rbcB"))

        def mmgroup(bank, mks, extra_first=()):
            n = len(mks)
            for i, mk in enumerate(mks):
                waits = ()
                if i == 0:
                    waits = list(bank.w_waits()) + list(extra_first)
                prog.pe(mk(i == 0, i == n - 1), waits=waits,
                        inc=(cPE if i == n - 1 else None))
            bank.set_ready(cPE)

        prog.dve(lambda e: e.memset(ones_cb[:], 1.0))
        prog.dve(lambda e: e.memset(ones_rb[:], 1.0))
        prog.dve(lambda e: e.memset(ones_sq[:], 1.0))
        prog.dve(lambda e: e.memset(ones_rf[:], 1.0), inc=cDVE)
        v_ones = cDVE.n

        prog.dma_g(lambda e: e.dma_start(cos_b[:], cosT[:, :]), ctr=cIC)
        prog.dma_g(lambda e: e.dma_start(sin_b[:], sinT[:, :]), ctr=cIC)
        for nm in bc:
            def mk(nm=nm):
                return lambda e: e.dma_start(bc[nm][:], bcin[nm][:, :])
            prog.dma_s(mk(), ctr=cBC)

        ra = rows[:, 0:2048]
        rb2 = rows[:, 2048:4096]
        prog.dma_s(lambda e: e.dma_start(ra, Wg_in[0:1, :]), ctr=cR)
        prog.dma_s(lambda e: e.dma_start(rb2, Wg_in[1:2, :]), ctr=cR)
        v = cR.n
        prog.dve(lambda e: e.tensor_tensor(ra, ra, rb2, ALU.subtract),
                 waits=[(cR, v)], inc=cDVE)
        R_row.add_read(cDVE)
        v = cDVE.n
        prog.dve(lambda e: e.tensor_reduce(sumw[:, 0:1], ra, AX, ALU.add), inc=cDVE)
        prog.dma_s(lambda e: e.dma_start(wdiff_dram.rearrange("(a d) -> a d", a=1), ra),
                   waits=[(cDVE, v)] + R_row.w_waits(), ctr=cR)
        v = cR.n
        prog.dma_g(lambda e: e.dma_start(wdiff_col[:],
                                         wdiff_dram.rearrange("(c p) -> p c", p=128),
                                         allow_slow_non_contiguous=True),
                   waits=[(cR, v)], ctr=cIC)
        v_wcol = cIC.n
        R_row.add_read(cIC)

        prog.dma_s(lambda e: e.dma_start(ra, b_o_r[:, :]), waits=R_row.w_waits(), ctr=cR)
        prog.dma_s(lambda e: e.dma_start(rb2, b1b_r[:, :]), ctr=cR)
        v = cR.n
        prog.dve(lambda e: e.tensor_tensor(row_sum_bf[:], ra, rb2, ALU.add),
                 waits=[(cR, v)], inc=cDVE)
        R_row.add_read(cDVE)
        v = cDVE.n
        prog.dma_s(lambda e: e.dma_start(ra, b0b_r[:, :]),
                   waits=[(cDVE, v)] + R_row.w_waits(), ctr=cR)
        v = cR.n
        prog.dve(lambda e: e.tensor_tensor(row_dif_bf[:], ra, rb2, ALU.subtract),
                 waits=[(cR, v)], inc=cDVE)
        R_row.add_read(cDVE)

        wc_done = {}
        wc_all = 0
        conv_jobs = []
        CvOUK, CvOUV, CvOUQ, CvOQR = 0, 256, 768, 1024
        if do_attn:
            for hgj in range(HGN):
                for ccj in range(CCH):
                    dmas = []
                    for off, wsrc, w0, w1 in ((CvOUK, W_uk, hgj * 256, 256),
                                              (CvOUV, W_uv, hgj * 512, 512),
                                              (CvOUQ, W_uq, hgj * 256, 256),
                                              (CvOQR, W_qr, hgj * 256, 256)):
                        def mk_l(off=off, wsrc=wsrc, w0=w0, w1=w1, hgj=hgj, ccj=ccj):
                            return lambda e: e.dma_start(
                                wattn_img[hgj, :, ccj * 1280 + off:ccj * 1280 + off + w1],
                                wsrc[ccj * 128:(ccj + 1) * 128, w0:w0 + w1])
                        dmas.append(mk_l())
                    conv_jobs.append(("wattn", dmas,
                                      hgj == HGN - 1 and ccj == CCH - 1))
        if do_ffn:
            for key, src4, dst4, nblk in (
                    ("W0a", W0a4, W0a4b, FCH),
                    ("W1a", W1a4, W1a4b, FCH),
                    ("Wo", Wo4, Wo4b, DCH),
                    ("W0b", W0b4, W0b4b, DCH),
                    ("W1b", W1b4, W1b4b, DCH)):
                for i in range(nblk):
                    def mk_l(i=i, src4=src4, dst4=dst4):
                        return lambda e: e.dma_start(
                            dst4[i].rearrange("p c f -> p (c f)"),
                            src4[i].rearrange("p c f -> p (c f)"))
                    conv_jobs.append((key, [mk_l()], i == nblk - 1))

        def pump(k):
            nonlocal wc_all
            while k > 0 and conv_jobs:
                key, dmas, last = conv_jobs.pop(0)
                for mk_l in dmas:
                    prog.dma_g(mk_l, ctr=cSc)
                if last:
                    wc_done[key] = cSc.n
                wc_all = cSc.n
                k -= 1

        tree_guard = [None, None]

        for s in range(n_seq):
            prog.join(ALLC)

            xn_tb = bfv(A_xn, 8192)
            wdkv_v = bfv(A_w, 8192)
            wdq_v = bfv(A_w + 8192, 8192)
            wkr_v = bfv(A_w + 16384, 16384)
            hst = [f32v(0, 512), f32v(512, 512)]
            hstR = [R["hstreamA"], R["hstreamB"]]
            h2 = [bfv(A_h2, 512), bfv(A_h2 + 512, 512)]
            h2R = [R["h2a"], R["h2b"]]
            hbf_all = bfv(43008, 8192)
            xtmp = f32v(1024, 512)
            rbv = f32v(1536, 512)
            mrbv = f32v(2048, 512)
            mean_r = rows[:, 0:512]
            msq_r = rows[:, 512:1024]
            var_r = rows[:, 1024:1536]
            rstd_r = rows[:, 1536:2048]
            mrb_r = rows[:, 2048:2560]
            gt_r = rows[:, 2560:3072]

            wv = R["warena"].w_waits()
            firstw = True
            for (w_v, wsrc, wid) in ((wdkv_v, W_dkv, DKV), (wdq_v, W_dq, DKV),
                                     (wkr_v, W_kr, 1024)):
                for dc in range(DCH):
                    def mk(dc=dc, w_v=w_v, wsrc=wsrc, wid=wid):
                        return lambda e: e.dma_start(w_v[:, dc * wid:(dc + 1) * wid],
                                                     wsrc[dc * 128:(dc + 1) * 128, :])
                    prog.dma_g(mk(), waits=(wv if firstw else ()), ctr=cW_)
                    firstw = False
            R["warena"].set_ready(cW_)
            v_w = cW_.n

            for tb in range(TBN):
                for dc in range(DCH):
                    b = dc % 2
                    def mk_h(dc=dc, b=b, s=s, tb=tb):
                        return lambda e: e.dma_start(hst[b][:], hT4[s, tb, dc])
                    prog.dma_s(mk_h(), waits=hstR[b].w_waits(), ctr=cH[b])
                    hstR[b].set_ready(cH[b])
                    def mk_sq(b=b):
                        return lambda e: e.activation(h2[b][:], hst[b][:], AF.Square)
                    prog.act(mk_sq(), waits=list(hstR[b].r_wait()) + h2R[b].w_waits(), inc=cACT)
                    h2R[b].set_ready(cACT)
                    def mk_cp(b=b, dc=dc):
                        return lambda e: e.tensor_copy(hbf_all[:, dc * TW:(dc + 1) * TW],
                                                       hst[b][:])
                    prog.dve(mk_cp(), waits=list(hstR[b].r_wait()) +
                             (R["hball"].w_waits() if dc == 0 else []), inc=cDVE)
                    v_cp = cDVE.n
                    st, sp = (dc == 0), (dc == DCH - 1)
                    w5 = pb[5].w_waits() if st else []
                    w6 = pb[6].w_waits() if st else []
                    w7 = pb[7].w_waits() if st else []
                    def mk_mean(dc=dc, st=st, sp=sp):
                        return lambda e: e.matmul(pb[5].t[0:1, :], ones_cb[:],
                                                  hbf_all[:, dc * TW:(dc + 1) * TW],
                                                  start=st, stop=sp)
                    prog.pe(mk_mean(), waits=w5 + [(cDVE, v_cp)], inc=cPE)
                    R["hball"].add_read(cPE)
                    def mk_msq(b=b, st=st, sp=sp):
                        return lambda e: e.matmul(pb[6].t[0:1, :], ones_cb[:], h2[b][:],
                                                  start=st, stop=sp)
                    prog.pe(mk_msq(), waits=w6 + list(h2R[b].r_wait()), inc=cPE)
                    h2R[b].add_read(cPE)
                    def mk_gt(dc=dc, b=b, st=st, sp=sp):
                        return lambda e: e.matmul(pb[7].t[0:1, :], wdiff_col[:, dc:dc + 1],
                                                  hst[b][:], start=st, stop=sp)
                    prog.pe(mk_gt(), waits=(w7 + [(cIC, v_wcol)]) if st else [], inc=cPE)
                    hstR[b].add_read(cPE)
                for b_ in (pb[5], pb[6], pb[7]):
                    b_.set_ready(cPE)
                prog.dve(lambda e: e.tensor_scalar(mean_r, pb[5].t[0:1, :], 1.0 / D, None, ALU.mult),
                         waits=pb[5].r_wait(), inc=cDVE)
                pb[5].add_read(cDVE)
                prog.dve(lambda e: e.tensor_scalar(msq_r, pb[6].t[0:1, :], 1.0 / D, None, ALU.mult),
                         waits=pb[6].r_wait(), inc=cDVE)
                pb[6].add_read(cDVE)
                prog.dve(lambda e: e.tensor_tensor(var_r, mean_r, mean_r, ALU.mult))
                prog.dve(lambda e: e.tensor_tensor(var_r, msq_r, var_r, ALU.subtract), inc=cDVE)
                prog.dve(lambda e: e.tensor_scalar(var_r, var_r, 1e-5, None, ALU.add), inc=cDVE)
                v = cDVE.n
                prog.act(lambda e: e.activation(var_r, var_r, AF.Ln), waits=[(cDVE, v)])
                prog.act(lambda e: e.activation(rstd_r, var_r, AF.Exp, scale=-0.5), inc=cACT)
                v_rstd = cACT.n
                prog.dve(lambda e: e.tensor_tensor(mrb_r, mean_r, rstd_r, ALU.mult),
                         waits=[(cACT, v_rstd)])
                prog.dve(lambda e: e.tensor_scalar(gt_r, mean_r, sumw[:, 0:1], None, ALU.mult))
                prog.dve(lambda e: e.tensor_tensor(gt_r, pb[7].t[0:1, :], gt_r, ALU.subtract),
                         waits=pb[7].r_wait())
                prog.dve(lambda e: e.tensor_tensor(gt_r, gt_r, rstd_r, ALU.mult))
                def mk_m(tb=tb):
                    return lambda e: e.tensor_scalar(m_row_bf[:, tb * TW:(tb + 1) * TW], gt_r,
                                                     0.0, None, ALU.is_ge)
                prog.dve(mk_m(), inc=cDVE)
                pb[7].add_read(cDVE)
                v_rows = cDVE.n
                def mk_b1(st, sp):
                    return lambda e: e.matmul(pb[4].t[:], ones_rf[:], rstd_r, start=True, stop=True)
                mmgroup(pb[4], [mk_b1], extra_first=[(cDVE, v_rows), (cDVE, v_ones)])
                prog.dve(lambda e: e.tensor_copy(rbv[:], pb[4].t[:]), waits=pb[4].r_wait(), inc=cDVE)
                pb[4].add_read(cDVE)
                def mk_b2(st, sp):
                    return lambda e: e.matmul(pb[4].t[:], ones_rf[:], mrb_r, start=True, stop=True)
                mmgroup(pb[4], [mk_b2])
                prog.dve(lambda e: e.tensor_copy(mrbv[:], pb[4].t[:]), waits=pb[4].r_wait(), inc=cDVE)
                pb[4].add_read(cDVE)
                v_bc = cDVE.n
                wx = R["xn"].w_waits()
                for dc in range(DCH):
                    def mk_a(dc=dc):
                        return lambda e: e.tensor_tensor(xtmp[:],
                                                         hbf_all[:, dc * TW:(dc + 1) * TW],
                                                         rbv[:], ALU.mult)
                    prog.dve(mk_a(), waits=[(cDVE, v_bc)] + (wx if dc == 0 else []))
                    def mk_b(dc=dc):
                        return lambda e: e.tensor_tensor(xn_tb[:, dc * TW:(dc + 1) * TW], xtmp[:],
                                                         mrbv[:], ALU.subtract)
                    prog.dve(mk_b(), inc=cDVE)
                    v_xn = cDVE.n
                    def mk_d(dc=dc, s=s, tb=tb):
                        return lambda e: e.dma_start(xn_dram[s, tb, dc],
                                                     xn_tb[:, dc * TW:(dc + 1) * TW])
                    prog.dma_s(mk_d(), waits=[(cDVE, v_xn)], ctr=cXS)
                R["xn"].set_ready(cDVE)
                R["xn"].add_read(cXS)
                R["hball"].add_read(cDVE)
                R["hball"].set_ready(cDVE)
                v_apply = cDVE.n

                for (w_v, bnm, dst, Rdst) in ((wdkv_v, "dkv", ckv_sb, R["ckv"]),
                                              (wdq_v, "dq", cq_sb, R["cq"])):
                    wdst = Rdst.w_waits() if tb == 0 else []
                    for cc in range(CCH):
                        bk = pb[cc % 2]
                        mks = []
                        for dc in range(DCH):
                            def mk_mm(cc=cc, dc=dc, bk=bk, w_v=w_v):
                                return lambda st, sp: (lambda e: e.matmul(
                                    bk.t[:], w_v[:, dc * DKV + cc * 128: dc * DKV + (cc + 1) * 128],
                                    xn_tb[:, dc * TW:(dc + 1) * TW], start=st, stop=sp))
                            mks.append(mk_mm())
                        mmgroup(bk, mks, extra_first=[(cW_, v_w), (cDVE, v_apply)] +
                                (wdst if cc == 0 else []))
                        R["xn"].add_read(cPE)
                        R["warena"].add_read(cPE)
                        def mk_ev(cc=cc, tb=tb, bk=bk, dst=dst, bnm=bnm):
                            return lambda e: e.activation(
                                dst[:, cc * S + tb * TW: cc * S + (tb + 1) * TW], bk.t[:],
                                AF.Identity, bias=bc[bnm][:, cc:cc + 1], scale=1.0)
                        prog.act(mk_ev(), waits=bk.r_wait(), inc=cACT)
                        bk.add_read(cACT)
                    if tb == TBN - 1:
                        Rdst.set_ready(cACT)

                for fc in range(8):
                    bk = pb[fc % 2]
                    mks = []
                    for dc in range(DCH):
                        def mk_mm(fc=fc, dc=dc, bk=bk):
                            return lambda st, sp: (lambda e: e.matmul(
                                bk.t[:], wkr_v[:, dc * 1024 + fc * 128: dc * 1024 + (fc + 1) * 128],
                                xn_tb[:, dc * TW:(dc + 1) * TW], start=st, stop=sp))
                        mks.append(mk_mm())
                    mmgroup(bk, mks, extra_first=[(cW_, v_w), (cDVE, v_apply)])
                    R["xn"].add_read(cPE)
                    R["warena"].add_read(cPE)
                    def mk_ev(fc=fc, bk=bk):
                        return lambda e: e.activation(kraw[:], bk.t[:], AF.Identity,
                                                      bias=bc["kr"][:, fc:fc + 1], scale=1.0)
                    prog.act(mk_ev(), waits=list(bk.r_wait()) + R["kraw"].w_waits(), inc=cACT)
                    bk.add_read(cACT)
                    R["kraw"].set_ready(cACT)
                    vk = cACT.n
                    for gi, (d0, s0, sc) in enumerate(((0, 32, -1.0), (32, 0, 1.0),
                                                       (64, 96, -1.0), (96, 64, 1.0))):
                        def mk_sw(d0=d0, s0=s0, sc=sc):
                            return lambda e: e.tensor_scalar(krot[d0:d0 + 32, :],
                                                             kraw[s0:s0 + 32, :], sc, None, ALU.mult)
                        prog.dve(mk_sw(), waits=([(cACT, vk)] if gi == 0 else ()))
                    def mk_t3(tb=tb):
                        return lambda e: e.tensor_tensor(t3_sb[:], kraw[:],
                                                         cos_b[:, tb * TW:(tb + 1) * TW], ALU.mult)
                    prog.dve(mk_t3())
                    def mk_t4(tb=tb):
                        return lambda e: e.tensor_tensor(t4_sb[:], krot[:],
                                                         sin_b[:, tb * TW:(tb + 1) * TW], ALU.mult)
                    prog.dve(mk_t4(), inc=cDVE)
                    R["kraw"].add_read(cDVE)
                    prog.dve(lambda e: e.tensor_tensor(t3_sb[:], t3_sb[:], t4_sb[:], ALU.add),
                             inc=cDVE)
                    v_t = cDVE.n
                    def mk_st(fc=fc, tb=tb):
                        return lambda e: e.dma_start(kr_dram[fc, :, tb * TW:(tb + 1) * TW], t3_sb[:])
                    prog.dma_s(mk_st(), waits=[(cDVE, v_t)], ctr=cKS)
                if s == 0 and do_ffn:
                    pump(14)
            xn_spill = cXS.n
            kr_spill = cKS.n
            if "gate" in dbg:
                prog.join(ALLC)
                def mk_g(s=s):
                    return lambda e: e.dma_start(dbg_aps["gate"][s], m_row_bf[:, :])
                prog.dma_s(mk_g(), ctr=cS)
                def mk_g2(s=s):
                    return lambda e: e.dma_start(dbg_aps["grow"][s], rows[:, :])
                prog.dma_s(mk_g2(), ctr=cS)
                prog.join(ALLC)
            if "xn" in dbg:
                prog.join(ALLC)
                for tb in range(TBN):
                    for dc in range(DCH):
                        def mk_l(s=s, tb=tb, dc=dc):
                            return lambda e: e.dma_start(kraw[:], xn_dram[s, tb, dc])
                        prog.dma_g(mk_l(), waits=[(cDVE, cDVE.n), (cS, cS.n)], ctr=cG)
                        v = cG.n
                        prog.dve(lambda e: e.tensor_copy(xtmp[:], kraw[:]), waits=[(cG, v)], inc=cDVE)
                        v = cDVE.n
                        def mk_s2(s=s, tb=tb, dc=dc):
                            return lambda e: e.dma_start(dbg_aps["xn"][s, tb, dc], xtmp[:])
                        prog.dma_s(mk_s2(), waits=[(cDVE, v)], ctr=cS)
                prog.join(ALLC)
            if "kr" in dbg:
                prog.join(ALLC)
                for fc in range(8):
                    for tb in range(TBN):
                        def mk_l(fc=fc, tb=tb):
                            return lambda e: e.dma_start(kraw[:], kr_dram[fc, :, tb * TW:(tb + 1) * TW])
                        prog.dma_g(mk_l(), waits=[(cDVE, cDVE.n), (cS, cS.n)], ctr=cG)
                        v = cG.n
                        prog.dve(lambda e: e.tensor_copy(xtmp[:], kraw[:]), waits=[(cG, v)], inc=cDVE)
                        v = cDVE.n
                        def mk_s2(fc=fc, tb=tb):
                            return lambda e: e.dma_start(dbg_aps["kr"][fc, :, tb * TW:(tb + 1) * TW],
                                                         xtmp[:])
                        prog.dma_s(mk_s2(), waits=[(cDVE, v)], ctr=cS)
                prog.join(ALLC)
            if "ckv" in dbg or "cq" in dbg:
                prog.join(ALLC)
                for (dkey, dst) in (("ckv", ckv_sb), ("cq", cq_sb)):
                    if dkey not in dbg:
                        continue
                    for cc in range(CCH):
                        for tb in range(TBN):
                            def mk_c(cc=cc, tb=tb, dst=dst):
                                return lambda e: e.tensor_copy(
                                    xtmp[:], dst[:, cc * S + tb * TW:cc * S + (tb + 1) * TW])
                            prog.dve(mk_c(), waits=[(cS, cS.n)], inc=cDVE)
                            v = cDVE.n
                            def mk_s2(cc=cc, tb=tb, dkey=dkey):
                                return lambda e: e.dma_start(
                                    dbg_aps[dkey][cc, :, tb * TW:(tb + 1) * TW], xtmp[:])
                            prog.dma_s(mk_s2(), waits=[(cDVE, v)], ctr=cS)
                prog.join(ALLC)

            if not do_attn:
                continue

            K_sb = bfv(T_K, HGS * S)
            V_sb = bfv(T_V, DCH * TW)
            Q_sb = bfv(T_Q, HGS * TW)
            P_v = [bfv(T_P0, DCH * TW), bfv(T_P1, DCH * TW)]
            PR = [R["P0"], R["P1"]]
            atn_sb = bfv(T_atn, HGS * TW)
            whgv = [bfv(T_whg, CCH * 1280), bfv(44032, CCH * 1280)]
            tree = bfv(T_tree, 4 * TW)
            ksq_sb = bfv(T_ksq, 2048)
            rbcv = [f32v(0, 512), f32v(1536, 512)]
            rbcR = [R["rbcA"], R["rbcB"]]
            buv_v = f32v(512, 512)
            xtmp_a = f32v(1024, 512)
            den = [rows[:, 0:512], rows[:, 1024:1536]]
            rec = [rows[:, 512:1024], rows[:, 1536:2048]]
            SC128 = float(1.0 / np.sqrt(128.0))

            def emit_scores(h4, hg, qb, k_act, k_dma, negc_v, qw=()):
                par = h4 % 2
                Pb, PbR = P_v[par], PR[par]
                wP = PbR.w_waits()
                ed = {}
                for kc in range(DCH):
                    Sb = pb[kc % 4]
                    def mk(kc=kc, h4=h4, Sb=Sb):
                        return lambda e: e.matmul(
                            Sb.t[:],
                            K_sb[:, h4 * S + kc * 128: h4 * S + (kc + 1) * 128],
                            Q_sb[:, h4 * TW:(h4 + 1) * TW], start=True, stop=True)
                    waits = list(Sb.w_waits())
                    if kc == 0:
                        waits += [(cACT, k_act), (cKR, k_dma)] + list(qw)
                    prog.pe(mk(), waits=waits, inc=cPE)
                    Sb.set_ready(cPE)
                    def mk_ex(kc=kc, h4=h4, Sb=Sb, Pb=Pb):
                        return lambda e: e.activation(
                            Pb[:, kc * TW:(kc + 1) * TW], Sb.t[:], AF.Exp,
                            bias=negc_sb[:, h4:h4 + 1], scale=SC128)
                    prog.act(mk_ex(), waits=list(Sb.r_wait()) +
                             ([(cDVE, negc_v)] + wP if kc == 0 else []),
                             inc=cACT)
                    Sb.add_read(cACT)
                    ed[kc // 2] = cACT.n
                PbR.set_ready(cACT)
                return ed

            def emit_fin1(h4, ed):
                par = h4 % 2
                Pb, PbR = P_v[par], PR[par]
                u = pb[6 + par]
                acc = [tree[:, (2 * par) * TW:(2 * par + 1) * TW],
                       tree[:, (2 * par + 1) * TW:(2 * par + 2) * TW]]
                w0 = [(cACT, ed[0])] + ([tree_guard[par]] if tree_guard[par] else [])
                def mk_a0(Pb=Pb, acc=acc):
                    return lambda e: e.tensor_tensor(acc[0], Pb[:, 0:TW],
                                                     Pb[:, TW:2 * TW], ALU.add)
                prog.dve(mk_a0(), waits=w0)
                for i in range(1, 15):
                    kc = i + 1
                    def mk_ai(i=i, kc=kc, Pb=Pb, acc=acc):
                        return lambda e: e.tensor_tensor(acc[i % 2], acc[(i - 1) % 2],
                                                         Pb[:, kc * TW:(kc + 1) * TW],
                                                         ALU.add)
                    prog.dve(mk_ai(), waits=[(cACT, ed[kc // 2])],
                             inc=(cDVE if i == 14 else None))
                v_tree = cDVE.n
                troot = acc[0]
                PbR.add_read(cDVE)
                def mk_dn(st, sp):
                    def f(e, troot=troot, u=u):
                        return e.matmul(u.t[:], ones_sq[:], troot[:],
                                        start=True, stop=True)
                    return f
                mmgroup(u, [mk_dn], extra_first=[(cDVE, v_tree)])
                tree_guard[par] = (cPE, cPE.n)
                def mk_ln(u=u):
                    return lambda e: e.activation(xtmp_a[:], u.t[:], AF.Ln)
                prog.act(mk_ln(), waits=u.r_wait())
                def mk_exr(par=par):
                    return lambda e: e.activation(rbcv[par][:], xtmp_a[:], AF.Exp,
                                                  scale=-1.0)
                prog.act(mk_exr(), waits=rbcR[par].w_waits(), inc=cACT)
                u.add_read(cACT)
                rbcR[par].set_ready(cACT)

            def emit_fin2(h4, ed, v_V):
                par = h4 % 2
                Pb, PbR = P_v[par], PR[par]
                obk = pb[4 + par]
                mks = []
                for kc in range(DCH):
                    def mk_pv(kc=kc, h4=h4, obk=obk, Pb=Pb):
                        return lambda st, sp: (lambda e: e.matmul(
                            obk.t[:], V_sb[:, kc * TW + h4 * 128: kc * TW + (h4 + 1) * 128],
                            Pb[:, kc * TW:(kc + 1) * TW], start=st, stop=sp))
                    mks.append(mk_pv())
                mmgroup(obk, mks, extra_first=[(cACT, ed[7]), (cDVE, v_V)])
                R["V"].add_read(cPE)
                PbR.add_read(cPE)
                R["K"].add_read(cPE)
                R["Q"].add_read(cPE)
                watn = R["atn"].w_waits() if h4 == 0 else []
                def mk_oev(h4=h4, obk=obk, par=par):
                    return lambda e: e.tensor_tensor(atn_sb[:, h4 * TW:(h4 + 1) * TW],
                                                     obk.t[:], rbcv[par][:], ALU.mult)
                prog.dve(mk_oev(), waits=list(obk.r_wait()) + list(rbcR[par].r_wait()) + watn,
                         inc=cDVE)
                obk.add_read(cDVE)
                rbcR[par].add_read(cDVE)

            OUK, OUV, OUQ, OQR = 0, 256, 768, 1024

            def load_whg(hg):
                Rw = R["whg%d" % (hg % 2)]
                wv = list(Rw.w_waits()) + [(cSc, wc_done["wattn"])]
                dst = whgv[hg % 2]
                def mk(hg=hg, dst=dst):
                    return lambda e: e.dma_start(dst[:], wattn_img[hg])
                prog.dma_s(mk(), waits=wv, ctr=cWhg)
                Rw.set_ready(cWhg)
                return cWhg.n

            while conv_jobs and conv_jobs[0][0] == "wattn":
                pump(1)
            prog.join(ALLC)
            v_whg_tab = {0: load_whg(0)}
            for hg in range(n_hg):
                whg = whgv[hg % 2]
                Rwhg = R["whg%d" % (hg % 2)]
                v_whg = v_whg_tab[hg]
                vb = prog.dma_s(lambda e: e.dma_start(rb2, b_uv_r[:, :]),
                                waits=R_row.w_waits(), ctr=cR)
                def mk_bv(st, sp):
                    def f(e, hg=hg):
                        return e.matmul(pb[6].t[:], ones_rf[:], rb2[:, hg * TW:(hg + 1) * TW],
                                        start=True, stop=True)
                    return f
                mmgroup(pb[6], [mk_bv], extra_first=[(cR, vb), (cDVE, v_ones)])
                R_row.add_read(cPE)
                prog.dve(lambda e: e.tensor_copy(buv_v[:], pb[6].t[:]), waits=pb[6].r_wait(),
                         inc=cDVE)
                pb[6].add_read(cDVE)
                v_buv = cDVE.n

                wK = R["K"].w_waits()
                for fc2 in range(2):
                    for tb in range(TBN):
                        bk = pb[4 + (fc2 * TBN + tb) % 2]
                        mks = []
                        for cc in range(CCH):
                            def mk_mm(fc2=fc2, tb=tb, cc=cc, bk=bk, whg=whg):
                                return lambda st, sp: (lambda e: e.matmul(
                                    bk.t[:],
                                    whg[:, cc * 1280 + OUK + fc2 * 128: cc * 1280 + OUK + (fc2 + 1) * 128],
                                    ckv_sb[:, cc * S + tb * TW: cc * S + (tb + 1) * TW],
                                    start=st, stop=sp))
                            mks.append(mk_mm())
                        mmgroup(bk, mks, extra_first=[(cWhg, v_whg)] + list(R["ckv"].r_wait()) +
                                ((wK + [(cXS, xn_spill)]) if fc2 == 0 and tb == 0 else []))
                        R["ckv"].add_read(cPE)
                        Rwhg.add_read(cPE)
                        for hh in range(2):
                            h4 = fc2 * 2 + hh
                            habs = hg * HGS + h4
                            bcol = (habs * 64) // 128
                            brow = (habs * 64) % 128
                            def mk_ev(tb=tb, hh=hh, h4=h4, bcol=bcol, brow=brow, bk=bk):
                                return lambda e: e.activation(
                                    K_sb[0:64, h4 * S + tb * TW: h4 * S + (tb + 1) * TW],
                                    bk.t[hh * 64:(hh + 1) * 64, :], AF.Identity,
                                    bias=bc["uk"][brow:brow + 64, bcol:bcol + 1], scale=1.0)
                            prog.act(mk_ev(), waits=(bk.r_wait() if hh == 0 else ()),
                                     inc=(cACT if hh == 1 else None))
                        bk.add_read(cACT)
                for h4 in range(HGS):
                    habs = hg * HGS + h4
                    fc = habs // 2
                    row = (habs % 2) * 64
                    def mk_l(h4=h4, fc=fc, row=row):
                        return lambda e: e.dma_start(K_sb[64:128, h4 * S:(h4 + 1) * S],
                                                     kr_dram[fc, row:row + 64, :])
                    prog.dma_s(mk_l(), waits=([(cKS, kr_spill)] if h4 == 0 else ()), ctr=cKR)
                R["K"].set_ready(cKR)
                k_act = cACT.n
                k_dma = cKR.n

                for h4 in range(HGS):
                    habs = hg * HGS + h4
                    u = pb[6 + h4 % 2]
                    def mk_sq(h4=h4):
                        return lambda e: e.activation(ksq_sb[:], K_sb[:, h4 * S:(h4 + 1) * S],
                                                      AF.Square)
                    prog.act(mk_sq(), waits=[(cACT, k_act), (cKR, k_dma)] + R["ksq"].w_waits(),
                             inc=cACT)
                    R["ksq"].set_ready(cACT)
                    v_ksq = cACT.n
                    for tb in range(TBN):
                        def mk_mm(tb=tb, u=u):
                            return lambda st, sp: (lambda e: e.matmul(
                                u.t[0:1, :], ones_cb[:], ksq_sb[:, tb * TW:(tb + 1) * TW],
                                start=True, stop=True))
                        mmgroup(u, [mk_mm()], extra_first=[(cACT, v_ksq)])
                        R["ksq"].add_read(cPE)
                        def mk_rd(tb=tb, u=u):
                            return lambda e: e.tensor_reduce(sc11[:, 8 + tb:9 + tb], u.t[0:1, :],
                                                             AX, ALU.max)
                        prog.dve(mk_rd(), waits=u.r_wait(), inc=cDVE)
                        u.add_read(cDVE)
                    prog.dve(lambda e: e.tensor_tensor(sc11[:, 8:9], sc11[:, 8:9], sc11[:, 9:10], ALU.max))
                    prog.dve(lambda e: e.tensor_tensor(sc11[:, 10:11], sc11[:, 10:11], sc11[:, 11:12], ALU.max))
                    prog.dve(lambda e: e.tensor_tensor(sc11[:, 8:9], sc11[:, 8:9], sc11[:, 10:11], ALU.max),
                             inc=cDVE)
                    v = cDVE.n
                    def mk_ln(habs=habs):
                        return lambda e: e.activation(klog[:, habs:habs + 1], sc11[:, 8:9], AF.Ln)
                    prog.act(mk_ln(), waits=[(cDVE, v)], inc=cACT)
                    v = cACT.n
                    def mk_adj(habs=habs):
                        return lambda e: e.tensor_scalar(klog[:, habs:habs + 1],
                                                         klog[:, habs:habs + 1], 0.5, NLN128,
                                                         ALU.mult, ALU.add)
                    prog.dve(mk_adj(), waits=[(cACT, v)], inc=cDVE)
                klog_dve = cDVE.n

                wV = R["V"].w_waits()
                for tt in range(DCH):
                    bk = pb[4 + tt % 2]
                    mks = []
                    for cc in range(CCH):
                        def mk_mm(tt=tt, cc=cc, bk=bk, whg=whg):
                            return lambda st, sp: (lambda e: e.matmul(
                                bk.t[:], ckv_sb[:, cc * S + tt * 128: cc * S + (tt + 1) * 128],
                                whg[:, cc * 1280 + OUV: cc * 1280 + OUV + 512], start=st, stop=sp))
                        mks.append(mk_mm())
                    mmgroup(bk, mks, extra_first=[(cWhg, v_whg), (cDVE, v_buv)] + (wV if tt == 0 else []))
                    R["ckv"].add_read(cPE)
                    Rwhg.add_read(cPE)
                    def mk_ev(tt=tt, bk=bk):
                        return lambda e: e.tensor_tensor(V_sb[:, tt * TW:(tt + 1) * TW], bk.t[:],
                                                         buv_v[:], ALU.add)
                    prog.dve(mk_ev(), waits=bk.r_wait(), inc=cDVE)
                    bk.add_read(cDVE)
                R["V"].set_ready(cDVE)
                v_V = cDVE.n
                if hg + 1 < n_hg:
                    v_whg_tab[hg + 1] = load_whg(hg + 1)

                for qb in range(n_qb):
                    wQ = R["Q"].w_waits()
                    for part, off in (("base", OUQ), ("rot", OQR)):
                        for fc2 in range(2):
                            bk = pb[4 + fc2 % 2]
                            mks = []
                            for cc in range(CCH):
                                def mk_mm(off=off, fc2=fc2, cc=cc, qb=qb, bk=bk, whg=whg):
                                    return lambda st, sp: (lambda e: e.matmul(
                                        bk.t[:],
                                        whg[:, cc * 1280 + off + fc2 * 128: cc * 1280 + off + (fc2 + 1) * 128],
                                        cq_sb[:, cc * S + qb * TW: cc * S + (qb + 1) * TW],
                                        start=st, stop=sp))
                                mks.append(mk_mm())
                            mmgroup(bk, mks, extra_first=[(cWhg, v_whg)] + list(R["cq"].r_wait()) +
                                    (wQ if part == "base" and fc2 == 0 else []))
                            R["cq"].add_read(cPE)
                            Rwhg.add_read(cPE)
                            if part == "base":
                                for hh in range(2):
                                    h4 = fc2 * 2 + hh
                                    habs = hg * HGS + h4
                                    bcol = (habs * 64) // 128
                                    brow = (habs * 64) % 128
                                    def mk_ev(hh=hh, h4=h4, bcol=bcol, brow=brow, bk=bk):
                                        return lambda e: e.activation(
                                            Q_sb[0:64, h4 * TW:(h4 + 1) * TW],
                                            bk.t[hh * 64:(hh + 1) * 64, :], AF.Identity,
                                            bias=bc["uq"][brow:brow + 64, bcol:bcol + 1], scale=1.0)
                                    prog.act(mk_ev(), waits=(bk.r_wait() if hh == 0 else ()),
                                             inc=(cACT if hh == 1 else None))
                                bk.add_read(cACT)
                            else:
                                fcg = hg * 2 + fc2
                                def mk_ev(fcg=fcg, bk=bk):
                                    return lambda e: e.activation(kraw[:], bk.t[:], AF.Identity,
                                                                  bias=bc["qr"][:, fcg:fcg + 1],
                                                                  scale=1.0)
                                prog.act(mk_ev(), waits=list(bk.r_wait()) + R["kraw"].w_waits(),
                                         inc=cACT)
                                bk.add_read(cACT)
                                R["kraw"].set_ready(cACT)
                                vk = cACT.n
                                for gi, (d0, s0, sc) in enumerate(((0, 32, -1.0), (32, 0, 1.0),
                                                                   (64, 96, -1.0), (96, 64, 1.0))):
                                    def mk_sw(d0=d0, s0=s0, sc=sc):
                                        return lambda e: e.tensor_scalar(
                                            krot[d0:d0 + 32, :], kraw[s0:s0 + 32, :], sc, None, ALU.mult)
                                    prog.dve(mk_sw(), waits=([(cACT, vk)] if gi == 0 else ()))
                                def mk_t3(qb=qb):
                                    return lambda e: e.tensor_tensor(
                                        t3_sb[:], kraw[:], cos_b[:, qb * TW:(qb + 1) * TW], ALU.mult)
                                prog.dve(mk_t3())
                                def mk_t4(qb=qb):
                                    return lambda e: e.tensor_tensor(
                                        t4_sb[:], krot[:], sin_b[:, qb * TW:(qb + 1) * TW], ALU.mult)
                                prog.dve(mk_t4(), inc=cDVE)
                                R["kraw"].add_read(cDVE)
                                for hh in range(2):
                                    h4 = fc2 * 2 + hh
                                    def mk_add(h4=h4, hh=hh):
                                        return lambda e: e.tensor_tensor(
                                            Q_sb[64:128, h4 * TW:(h4 + 1) * TW],
                                            t3_sb[hh * 64:(hh + 1) * 64, :],
                                            t4_sb[hh * 64:(hh + 1) * 64, :], ALU.add)
                                    prog.dve(mk_add(), inc=(cDVE if hh == 1 else None))
                    R["Q"].set_ready(cDVE)
                    q_act = cACT.n
                    q_dve = cDVE.n

                    prog.act(lambda e: e.activation(ksq_sb[:], Q_sb[:, 0:2048], AF.Square),
                             waits=[(cACT, q_act), (cDVE, q_dve)] + R["ksq"].w_waits(), inc=cACT)
                    R["ksq"].set_ready(cACT)
                    v_sq = cACT.n
                    negc_done = {}
                    for h4 in range(HGS):
                        u = pb[6 + h4 % 2]
                        def mk_qn(h4=h4, u=u):
                            return lambda st, sp: (lambda e: e.matmul(
                                u.t[0:1, :], ones_cb[:], ksq_sb[:, h4 * TW:(h4 + 1) * TW],
                                start=True, stop=True))
                        mmgroup(u, [mk_qn()], extra_first=[(cACT, v_sq)])
                        R["ksq"].add_read(cPE)
                        def mk_rd(h4=h4, u=u):
                            return lambda e: e.tensor_reduce(sc11[:, h4:h4 + 1], u.t[0:1, :],
                                                             AX, ALU.max)
                        prog.dve(mk_rd(), waits=u.r_wait(), inc=cDVE)
                        u.add_read(cDVE)
                    v = cDVE.n
                    h0c = hg * HGS
                    prog.act(lambda e: e.activation(sc11[:, 4:8], sc11[:, 0:4], AF.Ln),
                             waits=[(cDVE, v), (cDVE, klog_dve)], inc=cACT)
                    v = cACT.n
                    prog.dve(lambda e: e.tensor_scalar(sc11[:, 12:16], sc11[:, 4:8], 0.5,
                                                       None, ALU.mult), waits=[(cACT, v)])
                    def mk_comb(h0c=h0c):
                        return lambda e: e.tensor_tensor(sc11[:, 4:8], sc11[:, 12:16],
                                                         klog[:, h0c:h0c + 4], ALU.add)
                    prog.dve(mk_comb(), inc=cDVE)
                    v = cDVE.n
                    prog.act(lambda e: e.activation(sc11[:, 4:8], sc11[:, 4:8], AF.Exp),
                             waits=[(cDVE, v)], inc=cACT)
                    v = cACT.n
                    prog.dve(lambda e: e.tensor_scalar(sc11[:, 4:8], sc11[:, 4:8], -1.0,
                                                       SHIFT, ALU.mult, ALU.add),
                             waits=[(cACT, v)], inc=cDVE)
                    v = cDVE.n
                    for h4 in range(HGS):
                        u = pb[6 + h4 % 2]
                        def mk_nb(st, sp):
                            def f(e, h4=h4, u=u):
                                return e.matmul(u.t[:, 0:1], ones_rf[:], sc11[:, 4 + h4:5 + h4],
                                                start=True, stop=True)
                            return f
                        mmgroup(u, [mk_nb], extra_first=[(cDVE, v)])
                        def mk_cpn(h4=h4, u=u):
                            return lambda e: e.tensor_copy(negc_sb[:, h4:h4 + 1], u.t[:, 0:1])
                        prog.dve(mk_cpn(), waits=u.r_wait(), inc=cDVE)
                        u.add_read(cDVE)
                    for h4 in range(HGS):
                        negc_done[h4] = cDVE.n

                    eds = {}
                    for h4 in range(HGS):
                        eds[h4] = emit_scores(h4, hg, qb, k_act, k_dma, negc_done[h4],
                                              qw=[(cACT, q_act), (cDVE, q_dve)])
                        if h4 >= 1:
                            emit_fin1(h4 - 1, eds[h4 - 1])
                            emit_fin2(h4 - 1, eds[h4 - 1], v_V)
                    emit_fin1(HGS - 1, eds[HGS - 1])
                    emit_fin2(HGS - 1, eds[HGS - 1], v_V)

                    R["atn"].set_ready(cDVE)
                    for h4 in range(HGS):
                        def mk_st(s=s, qb=qb, hg=hg, h4=h4):
                            return lambda e: e.dma_start(attn_dram[s, qb, hg * HGS + h4],
                                                         atn_sb[:, h4 * TW:(h4 + 1) * TW])
                        prog.dma_s(mk_st(), waits=(R["atn"].r_wait() if h4 == 0 else ()), ctr=cAS)
                    R["atn"].add_read(cAS)
                if s == 0 and do_ffn:
                    pump(14)
            if do_ffn:
                pump(len(conv_jobs))
            attn_spill = cAS.n

            if "attn" in dbg:
                prog.join(ALLC)
                for qb in range(TBN):
                    for dc in range(DCH):
                        def mk_l(s=s, qb=qb, dc=dc):
                            return lambda e: e.dma_start(kraw[:], attn_dram[s, qb, dc])
                        prog.dma_g(mk_l(), waits=[(cDVE, cDVE.n), (cS, cS.n)], ctr=cG)
                        v = cG.n
                        prog.dve(lambda e: e.tensor_copy(xtmp_a[:], kraw[:]), waits=[(cG, v)],
                                 inc=cDVE)
                        v = cDVE.n
                        def mk_s2(s=s, qb=qb, dc=dc):
                            return lambda e: e.dma_start(dbg_aps["attn"][s, qb, dc], xtmp_a[:])
                        prog.dma_s(mk_s2(), waits=[(cDVE, v)], ctr=cS)
                prog.join(ALLC)

            if not do_ffn:
                continue

            if do_ffn:
                pump(len(conv_jobs))
            prog.join(ALLC)
            if "sent" in dbg:
                prog.dve(lambda e: e.memset(bfv(F_wf, 4096)[:], 111.0))
                prog.dve(lambda e: e.memset(bfv(F_wo, 4096)[:], 222.0))
                prog.dve(lambda e: e.memset(bfv(F_w0b, 2048)[:], 333.0))
                prog.dve(lambda e: e.memset(bfv(F_w1b, 2048)[:], 444.0))
                prog.dve(lambda e: e.memset(bfv(F_xa, 8192)[:], 555.0))
                prog.dve(lambda e: e.memset(bfv(F_h0, 16384)[:], 0.001))
                prog.dve(lambda e: e.memset(bfv(F_h1, 16384)[:], 0.002), inc=cDVE)
                prog.join(ALLC)
            xa_v = bfv(F_xa, 8192)
            h0m = bfv(F_h0, FCH * TW)
            h1m = bfv(F_h1, FCH * TW)
            wfv = [bfv(F_wf, 2048), bfv(F_wf + 2048, 2048),
                   bfv(F_wf2, 2048), bfv(F_wf2 + 2048, 2048)]
            wov = [bfv(F_wo, 2048), bfv(F_wo + 2048, 2048)]
            w0v = [bfv(F_w0b, 2048), bfv(F_w0b + 2048, 2048)]
            w1v = [bfv(F_w1b, 2048), bfv(F_w1b + 2048, 2048)]
            gv = [bfv(F_g, TW), bfv(F_g + TW, TW)]
            m_bc = bfv(F_mb, TW)
            m1_bc = f32v(2048, TW)
            hresv = [f32v(0, 512), f32v(512, 512)]
            outv = [f32v(1024, 512), f32v(1536, 512)]
            wfR = [R["wfA"], R["wfB"], R["wfC"], R["wfD"]]
            woR = [R["woA"], R["woB"]]
            w0R = [R["w0bA"], R["w0bB"]]
            w1R = [R["w1bA"], R["w1bB"]]
            gR = [R["gA"], R["gB"]]
            hrR = [R["hresA"], R["hresB"]]
            outR = [R["outA"], R["outB"]]

            for qb in range(n_qb):
                wx = R["xa"].w_waits()
                for dc in range(DCH):
                    def mk(dc=dc, s=s, qb=qb):
                        return lambda e: e.dma_start(xa_v[:, dc * TW:(dc + 1) * TW],
                                                     xn_dram[s, qb, dc])
                    prog.dma_s(mk(), waits=(wx + [(cXS, xn_spill), (cSc, wc_all)]
                                            if dc == 0 else ()), ctr=cXA)
                R["xa"].set_ready(cXA)
                v_x = cXA.n

                def mk_mb(st, sp):
                    def f(e, qb=qb):
                        return e.matmul(pb[6].t[:], ones_rb[:, 0:128],
                                        m_row_bf[:, qb * TW:(qb + 1) * TW], start=True, stop=True)
                    return f
                mmgroup(pb[6], [mk_mb], extra_first=R["mbc"].w_waits())
                prog.dve(lambda e: e.tensor_copy(m_bc[:], pb[6].t[:]), waits=pb[6].r_wait())
                prog.dve(lambda e: e.tensor_scalar(m1_bc[:], pb[6].t[:], -1.0, 1.0,
                                                   ALU.mult, ALU.add), inc=cDVE)
                pb[6].add_read(cDVE)
                R["mbc"].set_ready(cDVE)
                v_masks = cDVE.n

                for (w4b, wdone, bnm, func, hdst, Rh) in (
                        (W0a4b, wc_done.get("W0a", 0), "a0", AF.Gelu, h0m, R["h0"]),
                        (W1a4b, wc_done.get("W1a", 0), "a1", AF.Relu, h1m, R["h1"])):
                    wh = Rh.w_waits()
                    for fc in range(FCH):
                        b4 = fc % 4
                        q2 = fc % 2
                        b2 = fc % 2
                        def mk_lw(fc=fc, b4=b4, w4b=w4b):
                            return lambda e: e.dma_start(wfv[b4][:],
                                                         w4b[fc].rearrange("p c f -> p (c f)"))
                        (prog.dma_s if q2 == 0 else prog.dma_g)(
                            mk_lw(), waits=[(cSc, wc_all)] + wfR[b4].w_waits(), ctr=cWF[q2])
                        wfR[b4].set_ready(cWF[q2])
                        v_w = cWF[q2].n
                        bk = pb[2 + fc % 2]
                        mks = []
                        for dc in range(DCH):
                            def mk_mm(dc=dc, b4=b4, bk=bk):
                                return lambda st, sp: (lambda e: e.matmul(
                                    bk.t[:], wfv[b4][:, dc * 128:(dc + 1) * 128],
                                    xa_v[:, dc * TW:(dc + 1) * TW], start=st, stop=sp))
                            mks.append(mk_mm())
                        mmgroup(bk, mks, extra_first=[(cWF[q2], v_w), (cXA, v_x)] + (wh if fc == 0 else []))
                        R["xa"].add_read(cPE)
                        wfR[b4].add_read(cPE)
                        def mk_ga(fc=fc, b2=b2, bk=bk, bnm=bnm, func=func):
                            return lambda e: e.activation(gv[b2][:], bk.t[:], func,
                                                          bias=bc[bnm][:, fc:fc + 1], scale=1.0)
                        prog.act(mk_ga(), waits=list(bk.r_wait()) + gR[b2].w_waits(), inc=cACT)
                        bk.add_read(cACT)
                        gR[b2].set_ready(cACT)
                        mb = m_bc if func == AF.Gelu else m1_bc
                        def mk_mk(fc=fc, b2=b2, hdst=hdst, mb=mb):
                            return lambda e: e.tensor_tensor(hdst[:, fc * TW:(fc + 1) * TW],
                                                             gv[b2][:], mb[:], ALU.mult)
                        prog.dve(mk_mk(), waits=list(gR[b2].r_wait()) + [(cDVE, v_masks)], inc=cDVE)
                        gR[b2].add_read(cDVE)
                        R["mbc"].add_read(cDVE)
                    Rh.set_ready(cDVE)
                h_done = cDVE.n
                if "hm" in dbg and qb == 0:
                    for fc in range(FCH):
                        def mk_d0(s=s, fc=fc):
                            return lambda e: e.dma_start(dbg_aps["h0d"][s, fc], h0m[:, fc * TW:(fc + 1) * TW])
                        prog.dma_s(mk_d0(), ctr=cS)
                        def mk_d1(s=s, fc=fc):
                            return lambda e: e.dma_start(dbg_aps["h1d"][s, fc], h1m[:, fc * TW:(fc + 1) * TW])
                        prog.dma_s(mk_d1(), ctr=cS)
                    prog.join(ALLC)
                if "ffn" in dbg and s == 0 and qb == 0:
                    prog.join(ALLC)
                    prog.dma_s(lambda e: e.dma_start(dbg_aps["h0m"][:, :], h0m[:, :]), ctr=cS)
                    prog.dma_s(lambda e: e.dma_start(dbg_aps["h1m"][:, :], h1m[:, :]), ctr=cS)
                    prog.dma_s(lambda e: e.dma_start(dbg_aps["mbc"][:, 0:TW], m_bc[:]), ctr=cS)
                    prog.dma_s(lambda e: e.dma_start(dbg_aps["mbc"][:, TW:2 * TW], m1_bc[:]), ctr=cS)
                    prog.dma_s(lambda e: e.dma_start(dbg_aps["xa"][:, :], xa_v[:, :]), ctr=cS)
                    prog.join(ALLC)

                wx = R["xa"].w_waits()
                for dc in range(DCH):
                    def mk(dc=dc, s=s, qb=qb):
                        return lambda e: e.dma_start(xa_v[:, dc * TW:(dc + 1) * TW],
                                                     attn_dram[s, qb, dc])
                    prog.dma_s(mk(), waits=(wx + [(cAS, attn_spill)] if dc == 0 else ()), ctr=cXA)
                R["xa"].set_ready(cXA)
                v_a = cXA.n

                for do_ in range(DCH):
                    b2 = do_ % 2
                    def mk_lo(do_=do_, b2=b2):
                        return lambda e: e.dma_start(wov[b2][:],
                                                     Wo4b[do_].rearrange("p c f -> p (c f)"))
                    prog.dma_s(mk_lo(), waits=[(cSc, wc_all)] + woR[b2].w_waits(),
                               ctr=cWO[b2])
                    woR[b2].set_ready(cWO[b2])
                    v_wo = cWO[b2].n
                    hb = hresv[b2]
                    def mk_lr(do_=do_, hb=hb, s=s, qb=qb):
                        return lambda e: e.dma_start(hb[:], hT4[s, qb, do_])
                    prog.dma_g(mk_lr(), waits=hrR[b2].w_waits(), ctr=cHR[b2])
                    hrR[b2].set_ready(cHR[b2])
                    v_r = cHR[b2].n

                    bk = pb[2 + b2]
                    mks0 = []
                    for kc in range(DCH):
                        def mk_mm(kc=kc, b2=b2, bk=bk):
                            return lambda e: e.matmul(
                                bk.t[:], wov[b2][:, kc * 128:(kc + 1) * 128],
                                xa_v[:, kc * TW:(kc + 1) * TW], start=(kc == 0), stop=False)
                        mks0.append(mk_mm())
                    for i, mk in enumerate(mks0):
                        waits = []
                        if i == 0:
                            waits = bk.w_waits() + [(cWO[b2], v_wo), (cXA, v_a), (cDVE, h_done)]
                        prog.pe(mk, waits=waits, inc=(cPE if i == DCH - 1 else None))
                    R["xa"].add_read(cPE)
                    woR[b2].add_read(cPE)
                    for (hsrc, wvv, wRr, w4b, cPc, dmafn) in (
                            (h0m, w0v, w0R, W0b4b, cW0B, prog.dma_s),
                            (h1m, w1v, w1R, W1b4b, cW1B, prog.dma_g)):
                        for ph in range(2):
                            def mk_lp(do_=do_, ph=ph, wvv=wvv, w4b=w4b):
                                return lambda e: e.dma_start(
                                    wvv[ph][:],
                                    w4b[do_, :, ph * 16:(ph + 1) * 16, :].rearrange("p c f -> p (c f)"))
                            dmafn(mk_lp(), waits=[(cSc, wc_all)] + wRr[ph].w_waits(),
                                  ctr=cPc[ph])
                            wRr[ph].set_ready(cPc[ph])
                            v_wp = cPc[ph].n
                            for j in range(16):
                                fc = ph * 16 + j
                                def mk_mm(fc=fc, j=j, ph=ph, hsrc=hsrc, wvv=wvv, bk=bk):
                                    return lambda e: e.matmul(
                                        bk.t[:], wvv[ph][:, j * 128:(j + 1) * 128],
                                        hsrc[:, fc * TW:(fc + 1) * TW], start=False, stop=False)
                                prog.pe(mk_mm(), waits=([(cPc[ph], v_wp)] if j == 0 else []),
                                        inc=(cPE if j == 15 else None))
                            wRr[ph].add_read(cPE)
                    def mk_r1(do_=do_, bk=bk):
                        return lambda e: e.matmul(bk.t[:], row_sum_bf[:, do_ * 128:(do_ + 1) * 128],
                                                  ones_rb[:], start=False, stop=False)
                    prog.pe(mk_r1())
                    def mk_r2(do_=do_, qb=qb, bk=bk):
                        return lambda e: e.matmul(bk.t[:], row_dif_bf[:, do_ * 128:(do_ + 1) * 128],
                                                  m_row_bf[:, qb * TW:(qb + 1) * TW],
                                                  start=False, stop=True)
                    prog.pe(mk_r2(), inc=cPE)
                    bk.set_ready(cPE)
                    R["h0"].add_read(cPE)
                    R["h1"].add_read(cPE)
                    if "ostep" in dbg and qb == 0 and do_ == 0:
                        v_g = cPE.n
                        def mk_d1(s=s):
                            return lambda e: e.dma_start(dbg_aps["wov"][s], wov[0][:])
                        prog.dma_s(mk_d1(), waits=[(cPE, v_g)], ctr=cS)
                        def mk_d2(s=s):
                            return lambda e: e.dma_start(dbg_aps["w0v"][s, :, 0:1024], w0v[0][:, 0:1024])
                        prog.dma_s(mk_d2(), ctr=cS)
                        def mk_d3(s=s):
                            return lambda e: e.dma_start(dbg_aps["w0v"][s, :, 1024:2048], w0v[1][:, 0:1024])
                        prog.dma_s(mk_d3(), ctr=cS)
                        def mk_d4(s=s):
                            return lambda e: e.dma_start(dbg_aps["w1v"][s, :, 0:1024], w1v[0][:, 0:1024])
                        prog.dma_s(mk_d4(), ctr=cS)
                        def mk_d5(s=s):
                            return lambda e: e.dma_start(dbg_aps["w1v"][s, :, 1024:2048], w1v[1][:, 0:1024])
                        prog.dma_s(mk_d5(), ctr=cS)
                        v_d = cS.n
                        prog.emit("tensor", None, waits=[(cS, v_d)])
                    def mk_ev(b2=b2, bk=bk, hb=hb):
                        return lambda e: e.tensor_tensor(outv[b2][:], bk.t[:], hb[:], ALU.add)
                    prog.dve(mk_ev(), waits=list(bk.r_wait()) + [(cHR[b2], v_r)] +
                             list(hrR[b2].r_wait()) + outR[b2].w_waits(), inc=cDVE)
                    bk.add_read(cDVE)
                    hrR[b2].add_read(cDVE)
                    outR[b2].set_ready(cDVE)
                    def mk_so(s=s, qb=qb, do_=do_, b2=b2):
                        return lambda e: e.dma_start(outT[s, qb, do_], outv[b2][:])
                    prog.dma_s(mk_so(), waits=outR[b2].r_wait(), ctr=cOS[b2])
                    outR[b2].add_read(cOS[b2])

        if "dump" in dbg:
            prog.join(ALLC)
            for sd in range(n_seq):
                for tb in range(TBN):
                    for dc in range(DCH):
                        def mk1(sd=sd, tb=tb, dc=dc):
                            return lambda e: e.dma_start(dbg_aps["xnd"][sd, tb, dc],
                                                         xn_dram[sd, tb, dc])
                        prog.dma_g(mk1(), ctr=cG)
                        def mk2(sd=sd, tb=tb, dc=dc):
                            return lambda e: e.dma_start(dbg_aps["attnd"][sd, tb, dc],
                                                         attn_dram[sd, tb, dc])
                        prog.dma_g(mk2(), ctr=cG)
        finalw = [(c, c.n) for c in ALLC + (cSc, cGc)]
        prog.emit("sync", None, waits=finalw)
        prog.emit("gpsimd", None, waits=finalw)

        prog.replay(block)

    return nc



def _col(b):
    b = np.asarray(b, np.float32)
    return np.ascontiguousarray(b.reshape(-1, 128).T)


def _prep_inputs(ins):
    h = np.asarray(ins["L0_attn_h"], np.float32)
    f32 = lambda x: np.ascontiguousarray(np.asarray(x, np.float32))

    W0aT = np.ascontiguousarray(f32(ins["L0_ffn_W0a"]).T)
    W1aT = np.ascontiguousarray(f32(ins["L0_ffn_W1a"]).T)
    W0bT = np.ascontiguousarray(f32(ins["L0_ffn_W0b"]).T)
    W1bT = np.ascontiguousarray(f32(ins["L0_ffn_W1b"]).T)
    Wo = f32(ins["L0_attn_W_o"])

    W0a4 = np.ascontiguousarray(W0aT.reshape(DCH, 128, FCH, 128).transpose(2, 1, 0, 3))
    W1a4 = np.ascontiguousarray(W1aT.reshape(DCH, 128, FCH, 128).transpose(2, 1, 0, 3))
    W0b4 = np.ascontiguousarray(W0bT.reshape(FCH, 128, DCH, 128).transpose(2, 1, 0, 3))
    W1b4 = np.ascontiguousarray(W1bT.reshape(FCH, 128, DCH, 128).transpose(2, 1, 0, 3))
    Wo4 = np.ascontiguousarray(Wo.reshape(DCH, 128, DCH, 128).transpose(2, 1, 0, 3))

    d_rope = 64
    inv_freq = (1.0 / (10000.0 ** (np.arange(0, d_rope, 2, dtype=np.float64) / d_rope)))
    t = np.arange(S, dtype=np.float64)
    freqs = t[:, None] * inv_freq[None, :]
    emb = np.concatenate([freqs, freqs], axis=-1)
    cosT = np.ascontiguousarray(np.tile(np.cos(emb).T.astype(np.float32), (2, 1)))
    sinT = np.ascontiguousarray(np.tile(np.sin(emb).T.astype(np.float32), (2, 1)))

    rowf = lambda b: np.ascontiguousarray(np.asarray(b, np.float32).reshape(1, -1))

    shared = dict(
        W_dkv=f32(ins["L0_attn_W_dkv"]), W_dq=f32(ins["L0_attn_W_dq"]),
        W_uk=f32(ins["L0_attn_W_uk"]), W_uv=f32(ins["L0_attn_W_uv"]),
        W_uq=f32(ins["L0_attn_W_uq"]), W_qr=f32(ins["L0_attn_W_qr"]),
        W_kr=f32(ins["L0_attn_W_kr"]),
        W0a4=W0a4, W1a4=W1a4, W0b4=W0b4, W1b4=W1b4, Wo4=Wo4,
        bc_dkv=_col(ins["L0_attn_b_dkv"]), bc_dq=_col(ins["L0_attn_b_dq"]),
        bc_uk=_col(ins["L0_attn_b_uk"]), bc_uq=_col(ins["L0_attn_b_uq"]),
        bc_qr=_col(ins["L0_attn_b_qr"]), bc_kr=_col(ins["L0_attn_b_kr"]),
        bc_a0=_col(ins["L0_ffn_b0a"]), bc_a1=_col(ins["L0_ffn_b1a"]),
        b_uv_r=rowf(ins["L0_attn_b_uv"]), b_o_r=rowf(ins["L0_attn_b_o"]),
        b0b_r=rowf(ins["L0_ffn_b0b"]), b1b_r=rowf(ins["L0_ffn_b1b"]),
        Wg=f32(ins["L0_ffn_Wg"]), cosT=cosT, sinT=sinT,
    )

    in_maps = []
    for c in range(8):
        hc = h[2 * c:2 * c + 2]
        hT = hc.transpose(0, 2, 1)
        hT4 = np.ascontiguousarray(hT.reshape(2, DCH, 128, TBN, TW).transpose(0, 3, 1, 2, 4))
        m = dict(shared)
        m["hT4"] = hT4
        in_maps.append(m)
    return in_maps


def _assemble_output(results):
    outs = []
    for c in range(8):
        o = results[c]["outT"]
        o = o.transpose(0, 1, 4, 2, 3).reshape(2, S, D)
        outs.append(o)
    return np.ascontiguousarray(np.concatenate(outs, axis=0)).astype(np.float32)


def kernel(**inputs):
    in_maps = _prep_inputs(inputs)
    nc = build_program()
    res = run_bass_kernel_spmd(nc, in_maps, core_ids=list(range(8)))
    return _assemble_output(res.results)
